# version 05
# baseline (speedup 1.0000x reference)
import contextlib
import numpy as np
import ml_dtypes
import concourse.bass as bass
import concourse.mybir as mybir
from concourse.bass_utils import run_bass_kernel_spmd

F32 = mybir.dt.float32
BF16 = mybir.dt.bfloat16
AF = mybir.ActivationFunctionType
ALU = mybir.AluOpType
AX = mybir.AxisListType

R = 8
D = 1024
TOK = 4096
NSUB = TOK // 128

PHASES_A = ["p1", "p2", "p3"]
PHASES_B = ["p4", "p5", "p6"]
MOE_EXPERTS = 8


class _Sem:
    def __init__(self, nc, name, unordered=False):
        self.h = nc.alloc_semaphore(name)
        self.count = 0
        self.unordered = unordered


class Buf:
    __slots__ = ("prod", "prod_eng", "readers", "name")

    def __init__(self, name=""):
        self.prod = None
        self.prod_eng = None
        self.readers = []
        self.name = name


class _Eng:
    def __init__(self, nc, engine, name, unordered=False):
        self.e = engine
        self.name = name
        self.sem = _Sem(nc, f"s_{name}", unordered=unordered)
        self.waited = {}

    def wait_ms(self, ms):
        sem, val = ms
        if sem is self.sem and self.name == "pe":
            return
        if sem.unordered:
            val = sem.count
        if self.waited.get(id(sem), 0) >= val:
            return
        assert val <= sem.count, "wait on future sem value"
        self.e.wait_ge(sem.h, val)
        self.waited[id(sem)] = val


class Ctx:
    def __init__(self, nc):
        self.nc = nc
        self.pe = _Eng(nc, nc.tensor, "pe")
        self.dve = _Eng(nc, nc.vector, "dve")
        self.act = _Eng(nc, nc.scalar, "act")
        self.pool = _Eng(nc, nc.gpsimd, "pool", unordered=True)
        self.sp = _Eng(nc, nc.sync, "sp", unordered=True)
        self.engines = [self.pe, self.dve, self.act, self.pool, self.sp]
        self.all_sems = [e.sem for e in self.engines]

    def stream_sem(self, name):
        s = _Sem(self.nc, name, unordered=True)
        self.all_sems.append(s)
        return s

    def emit(self, eng, fn, reads=(), writes=(), sreads=(), inc=1, sem=None):
        for b in sreads:
            if b.prod is not None:
                assert b.prod_eng is not eng, (
                    f"scalar-ptr operand {b.name} produced on same engine {eng.name}")
        for b in list(reads) + list(sreads):
            if b.prod is not None:
                eng.wait_ms(b.prod)
        for b in writes:
            if b.prod is not None:
                eng.wait_ms(b.prod)
            for ms in b.readers:
                eng.wait_ms(ms)
        instr = fn(eng.e)
        s = sem if sem is not None else eng.sem
        instr.then_inc(s.h, inc)
        s.count += inc
        ms = (s, s.count)
        for b in writes:
            b.prod = ms
            b.prod_eng = eng
            b.readers = []
        for b in list(reads) + list(sreads):
            b.readers.append(ms)
        return instr

    def mm_group(self, fns, reads=(), writes=()):
        eng = self.pe
        for b in reads:
            if b.prod is not None:
                eng.wait_ms(b.prod)
        for b in writes:
            if b.prod is not None:
                eng.wait_ms(b.prod)
            for ms in b.readers:
                eng.wait_ms(ms)
        instr = None
        for fn in fns:
            instr = fn(self.nc.tensor)
        instr.then_inc(eng.sem.h, 1)
        eng.sem.count += 1
        ms = (eng.sem, eng.sem.count)
        for b in writes:
            b.prod = ms
            b.prod_eng = eng
            b.readers = []
        for b in reads:
            b.readers.append(ms)
        return instr

    @contextlib.contextmanager
    def scope(self):
        parent = getattr(self, "_stack", None)
        with contextlib.ExitStack() as stack:
            self._stack = stack
            yield stack
            self.barrier_all()
        self._stack = parent

    def sb(self, stack, name, shape, dtype):
        return stack.enter_context(self.nc.sbuf_tensor(name, shape, dtype))

    def psum(self, stack, name, shape, dtype):
        return stack.enter_context(self.nc.psum_tensor(name, shape, dtype))

    def barrier_all(self):
        for e1 in self.engines:
            for s in self.all_sems:
                if s is not e1.sem and s.count > 0:
                    e1.wait_ms((s, s.count))


def _layernorm_subtile(ctx, xt, xt_b, stats, xln_dst_col, xlnfm, xlnfm_b, ident, ident_b,
                      ptr, ptr_b, out_bf16=True):
    nc = ctx.nc
    s = stats
    ctx.emit(ctx.dve, lambda e: e.reduce_sum(s["sum"][0][:, :], xt[:, :], axis=AX.X),
             reads=[xt_b], writes=[s["sum"][1]])
    ctx.emit(ctx.dve, lambda e: e.tensor_tensor(s["xsq"][0][:, :], xt[:, :], xt[:, :], ALU.mult),
             reads=[xt_b], writes=[s["xsq"][1]])
    ctx.emit(ctx.dve, lambda e: e.reduce_sum(s["ssq"][0][:, :], s["xsq"][0][:, :], axis=AX.X),
             reads=[s["xsq"][1]], writes=[s["ssq"][1]])
    ctx.emit(ctx.act, lambda e: e.activation(s["nmean"][0][:, :], s["sum"][0][:, :], AF.Copy,
                                             scale=-1.0 / D),
             reads=[s["sum"][1]], writes=[s["nmean"][1]])
    ctx.emit(ctx.act, lambda e: e.activation(s["msq"][0][:, :], s["sum"][0][:, :], AF.Square,
                                             scale=1.0 / D),
             reads=[s["sum"][1]], writes=[s["msq"][1]])
    ctx.emit(ctx.dve, lambda e: e.tensor_scalar(s["sqv"][0][:, :], s["ssq"][0][:, :],
                                                1.0 / D, 1e-5, ALU.mult, ALU.add),
             reads=[s["ssq"][1]], writes=[s["sqv"][1]])
    ctx.emit(ctx.dve, lambda e: e.tensor_tensor(s["var"][0][:, :], s["sqv"][0][:, :],
                                                s["msq"][0][:, :], ALU.subtract),
             reads=[s["sqv"][1], s["msq"][1]], writes=[s["var"][1]])
    ctx.emit(ctx.act, lambda e: e.activation(s["sq"][0][:, :], s["var"][0][:, :], AF.Sqrt),
             reads=[s["var"][1]], writes=[s["sq"][1]])
    ctx.emit(ctx.dve, lambda e: e.reciprocal(s["rstd"][0][:, :], s["sq"][0][:, :]),
             reads=[s["sq"][1]], writes=[s["rstd"][1]])
    ctx.emit(ctx.dve, lambda e: e.tensor_tensor(s["mb"][0][:, :], s["nmean"][0][:, :],
                                                s["rstd"][0][:, :], ALU.mult),
             reads=[s["nmean"][1], s["rstd"][1]], writes=[s["mb"][1]])
    ctx.emit(ctx.act, lambda e: e.activation(s["xln_tm"][0][:, :], xt[:, :], AF.Identity,
                                             scale=s["rstd"][0][:, :], bias=s["mb"][0][:, :]),
             reads=[xt_b], sreads=[s["rstd"][1], s["mb"][1]], writes=[s["xln_tm"][1]])
    fns = [
        (lambda e, kk=k: e.transpose(ptr[:, kk, :],
                                     s["xln_tm"][0][:, kk * 128:kk * 128 + 128],
                                     ident[:, :]))
        for k in range(8)
    ]
    ctx.mm_group(fns, reads=[s["xln_tm"][1], ident_b], writes=[ptr_b])
    ctx.emit(ctx.dve, lambda e: e.tensor_copy(
        xlnfm[:, :, xln_dst_col:xln_dst_col + 128], ptr[:, :, :]),
        reads=[ptr_b], writes=[xlnfm_b])


def _alloc_ln_stats(ctx, tag, shared_xsq=None):
    names = ["sum", "xsq", "ssq", "nmean", "msq", "sqv", "var", "sq", "rstd", "mb", "xln_tm"]
    out = {}
    for n in names:
        if n == "xsq":
            if shared_xsq is not None:
                out[n] = shared_xsq
                continue
            t = ctx.sb(ctx._stack, f"{tag}_{n}", [128, D], F32)
        elif n == "xln_tm":
            t = ctx.sb(ctx._stack, f"{tag}_{n}", [128, D], BF16)
        else:
            t = ctx.sb(ctx._stack, f"{tag}_{n}", [128, 1], F32)
        out[n] = (t, Buf(f"{tag}_{n}"))
    return out


def _alloc_ln_slots(ctx, tag):
    s0 = _alloc_ln_stats(ctx, f"{tag}0")
    s1 = _alloc_ln_stats(ctx, f"{tag}1", shared_xsq=s0["xsq"])
    return [s0, s1]


def _bcast_row_to_partitions(ctx, src_row_ap, src_b, ones, ones_b, ps, ps_b, dst, dst_b,
                             ncols, scale=None):
    nc = ctx.nc
    nmm = (ncols + 511) // 512
    for i in range(nmm):
        c0, c1 = i * 512, min(ncols, (i + 1) * 512)
        w = c1 - c0
        ctx.mm_group([lambda e, c0=c0, c1=c1, w=w: e.matmul(ps[:, 0:w], ones[0:1, 0:128],
                                                            src_row_ap[0:1, c0:c1],
                                                            start=True, stop=True)],
                     reads=[src_b, ones_b], writes=[ps_b])
        if scale is None:
            ctx.emit(ctx.dve, lambda e, c0=c0, c1=c1, w=w: e.tensor_copy(dst[:, c0:c1], ps[:, 0:w]),
                     reads=[ps_b], writes=[dst_b])
        else:
            ctx.emit(ctx.dve, lambda e, c0=c0, c1=c1, w=w: e.tensor_scalar(
                dst[:, c0:c1], ps[:, 0:w], scale, None, ALU.mult),
                reads=[ps_b], writes=[dst_b])


def build_launch_A():
    nc = bass.Bass("TRN2", target_bir_lowering=False, debug=False)

    xin = nc.dram_tensor("xin", [TOK, D], F32, kind="ExternalInput")
    w1t = nc.dram_tensor("w1t", [D, 4096], F32, kind="ExternalInput")
    w2t = nc.dram_tensor("w2t", [4096, D], F32, kind="ExternalInput")
    b1 = nc.dram_tensor("b1", [1, 4096], F32, kind="ExternalInput")
    b2 = nc.dram_tensor("b2", [1, D], F32, kind="ExternalInput")
    wk = nc.dram_tensor("wk", [D, D], F32, kind="ExternalInput")
    wq = nc.dram_tensor("wq", [D, D], F32, kind="ExternalInput")
    wo = nc.dram_tensor("wo", [D, D], F32, kind="ExternalInput")
    bo = nc.dram_tensor("bo", [1, D], F32, kind="ExternalInput")
    ident = nc.dram_tensor("ident", [128, 128], BF16, kind="ExternalInput")
    onesf = nc.dram_tensor("onesf", [128, 128], F32, kind="ExternalInput")

    xout = nc.dram_tensor("xout", [TOK, D], F32, kind="ExternalOutput")

    w1t_bf = nc.dram_tensor("w1t_bf", [D, 4096], BF16)
    w2t_bf = nc.dram_tensor("w2t_bf", [4096, D], BF16)
    xs1 = nc.dram_tensor("xs1", [TOK, D], F32)
    xs2 = nc.dram_tensor("xs2", [TOK, D], F32)
    xln_d = nc.dram_tensor("xln_d", [D, TOK], BF16)

    ctx = Ctx(nc)

    with nc.Block():
        id_sb = nc.alloc_sbuf_tensor("id_sb", [128, 128], BF16)
        ones_sb = nc.alloc_sbuf_tensor("ones_sb", [128, 128], F32)
        id_b, ones_b = Buf("id"), Buf("ones")
        ctx.emit(ctx.sp, lambda e: e.dma_start(out=id_sb[:, :], in_=ident[:, :]),
                 writes=[id_b], inc=16)
        ctx.emit(ctx.sp, lambda e: e.dma_start(out=ones_sb[:, :], in_=onesf[:, :]),
                 writes=[ones_b], inc=16)

        with ctx.scope():
            _p0_ternarize(ctx, w1t, w1t_bf, D, 4096, ones_sb, ones_b, half=False, tag="t1")
            _p0_ternarize(ctx, w2t, w2t_bf, 4096, D, ones_sb, ones_b, half=True, tag="t2")

        if "p1" in PHASES_A:
            with ctx.scope():
                _ffn_phase(ctx, xin, xs1 if ("p2" in PHASES_A or "p3" in PHASES_A) else xout,
                           w1t_bf, w2t_bf, b1, b2, id_sb, id_b, ones_sb, ones_b, xln_d,
                           tag="f1")
        if "p2" in PHASES_A:
            src = xs1 if "p1" in PHASES_A else xin
            dst = xs2 if "p3" in PHASES_A else xout
            with ctx.scope():
                _attn_phase(ctx, src, dst, wk, wq, wo, bo, id_sb, id_b, ones_sb, ones_b,
                            xln_d)
        if "p3" in PHASES_A:
            src = xs2 if "p2" in PHASES_A else (xs1 if "p1" in PHASES_A else xin)
            with ctx.scope():
                _ffn_phase(ctx, src, xout, w1t_bf, w2t_bf, b1, b2, id_sb, id_b, ones_sb,
                           ones_b, xln_d, tag="f2")

        ctx.barrier_all()
    return nc


def _p0_ternarize(ctx, wsrc, wdst, rows, cols, ones_sb, ones_b, half, tag):
    nc = ctx.nc
    stack = ctx._stack
    nch = rows // 128
    slab = [ctx.sb(stack, f"{tag}_slab{i}", [128, cols], F32) for i in range(2)]
    slab_b = [Buf(f"{tag}_slab{i}") for i in range(2)]
    absw = ctx.sb(stack, f"{tag}_absw", [128, cols], F32)
    absw_b = Buf(f"{tag}_absw")
    sgn = [ctx.sb(stack, f"{tag}_sgn{i}", [128, cols], F32) for i in range(2)]
    sgn_b = [Buf(f"{tag}_sgn{i}") for i in range(2)]
    mask = ctx.sb(stack, f"{tag}_mask", [128, cols], F32)
    mask_b = Buf(f"{tag}_mask")
    obf = [ctx.sb(stack, f"{tag}_obf{i}", [128, cols], BF16) for i in range(2)]
    obf_b = [Buf(f"{tag}_obf{i}") for i in range(2)]
    red = [ctx.sb(stack, f"{tag}_red{i}", [128, 1], F32) for i in range(2)]
    red_b = [Buf(f"{tag}_red{i}") for i in range(2)]
    part = ctx.sb(stack, f"{tag}_part", [128, 1], F32)
    part_b = Buf(f"{tag}_part")
    tot = ctx.sb(stack, f"{tag}_tot", [1, 1], F32)
    tot_b = Buf(f"{tag}_tot")
    thr = ctx.sb(stack, f"{tag}_thr", [128, 1], F32)
    thr_b = Buf(f"{tag}_thr")
    pss = ctx.psum(stack, f"{tag}_ps", [128, 1], F32)
    pss_b = Buf(f"{tag}_ps")

    ctx.emit(ctx.dve, lambda e: e.memset(part[:, :], 0.0), writes=[part_b])
    for c in range(nch):
        cc = c
        par = c % 2
        q = ctx.sp if (c % 2 == 0) else ctx.pool
        ctx.emit(q, lambda e, cc=cc, par=par: e.dma_start(
            out=slab[par][:, :], in_=wsrc[cc * 128:cc * 128 + 128, :]),
            writes=[slab_b[par]], inc=16)
        ctx.emit(ctx.dve, lambda e, par=par: e.reduce_sum(
            red[par][:, :], slab[par][:, :], axis=AX.X, apply_absolute_value=True),
            reads=[slab_b[par]], writes=[red_b[par]])
        ctx.emit(ctx.dve, lambda e, par=par: e.tensor_tensor(
            part[:, :], part[:, :], red[par][:, :], ALU.add),
            reads=[red_b[par]], writes=[part_b])
    ctx.mm_group([lambda e: e.matmul(pss[0:1, 0:1], part[:, 0:1], ones_sb[:, 0:1],
                                     start=True, stop=True)],
                 reads=[part_b, ones_b], writes=[pss_b])
    numel = rows * cols
    ctx.emit(ctx.act, lambda e: e.activation(tot[:, :], pss[0:1, 0:1], AF.Copy,
                                             scale=0.5 / numel),
             reads=[pss_b], writes=[tot_b])
    ctx.emit(ctx.dve, lambda e: e.tensor_scalar(tot[:, :], tot[:, :], 0.5e-8, None, ALU.max),
             reads=[tot_b], writes=[tot_b])
    ctx.mm_group([lambda e: e.matmul(pss[:, 0:1], ones_sb[0:1, 0:128], tot[0:1, 0:1],
                                     start=True, stop=True)],
                 reads=[tot_b, ones_b], writes=[pss_b])
    ctx.emit(ctx.act, lambda e: e.activation(thr[:, :], pss[:, 0:1], AF.Copy),
             reads=[pss_b], writes=[thr_b])
    for c in range(nch):
        cc = c
        par = c % 2
        ql = ctx.sp if (c % 2 == 0) else ctx.pool
        qs = ctx.pool if (c % 2 == 0) else ctx.sp
        ctx.emit(ql, lambda e, cc=cc, par=par: e.dma_start(
            out=slab[par][:, :], in_=wsrc[cc * 128:cc * 128 + 128, :]),
            writes=[slab_b[par]], inc=16)
        ctx.emit(ctx.act, lambda e, par=par: e.activation(sgn[par][:, :], slab[par][:, :],
                                                          AF.Sign),
                 reads=[slab_b[par]], writes=[sgn_b[par]])
        ctx.emit(ctx.act, lambda e, par=par: e.activation(absw[:, :], slab[par][:, :], AF.Abs),
                 reads=[slab_b[par]], writes=[absw_b])
        ctx.emit(ctx.dve, lambda e: e.tensor_scalar(mask[:, :], absw[:, :], thr[:, :], None,
                                                    ALU.is_ge),
                 reads=[absw_b], sreads=[thr_b], writes=[mask_b])
        if half:
            ctx.emit(ctx.dve, lambda e, par=par: e.tensor_tensor(
                mask[:, :], mask[:, :], sgn[par][:, :], ALU.mult),
                reads=[sgn_b[par]], writes=[mask_b])
            ctx.emit(ctx.dve, lambda e, par=par: e.tensor_scalar(
                obf[par][:, :], mask[:, :], 0.5, None, ALU.mult),
                reads=[mask_b], writes=[obf_b[par]])
        else:
            ctx.emit(ctx.dve, lambda e, par=par: e.tensor_tensor(
                obf[par][:, :], mask[:, :], sgn[par][:, :], ALU.mult),
                reads=[mask_b, sgn_b[par]], writes=[obf_b[par]])
        ctx.emit(qs, lambda e, cc=cc, par=par: e.dma_start(
            out=wdst[cc * 128:cc * 128 + 128, :], in_=obf[par][:, :]),
            reads=[obf_b[par]], writes=[], inc=16)


def _ffn_phase(ctx, xsrc, xdst, w1t_bf, w2t_bf, b1, b2, id_sb, id_b, ones_sb, ones_b,
               xln_d, tag):
    nc = ctx.nc
    stack = ctx._stack
    s_wl = ctx.stream_sem(f"{tag}_wl")
    s_xl = ctx.stream_sem(f"{tag}_xl")
    s_st = ctx.stream_sem(f"{tag}_st")
    wsb1 = ctx.sb(stack, f"{tag}_wsb1", [128, 8, 4096], BF16)
    wsb1_b = Buf(f"{tag}_wsb1")
    wsb2 = ctx.sb(stack, f"{tag}_wsb2", [128, 32, D], BF16)
    wsb2_b = Buf(f"{tag}_wsb2")
    for k in range(8):
        kk = k
        q = ctx.sp if (k % 2 == 0) else ctx.pool
        ctx.emit(q, lambda e, kk=kk: e.dma_start(out=wsb1[:, kk, :],
                                                 in_=w1t_bf[kk * 128:kk * 128 + 128, :]),
                 writes=[wsb1_b], inc=16, sem=s_wl)
    for fc in range(32):
        ff = fc
        q = ctx.sp if (fc % 2 == 0) else ctx.pool
        ctx.emit(q, lambda e, ff=ff: e.dma_start(out=wsb2[:, ff, :],
                                                 in_=w2t_bf[ff * 128:ff * 128 + 128, :]),
                 writes=[wsb2_b], inc=16, sem=s_wl)

    b1sb = ctx.sb(stack, f"{tag}_b1sb", [128, 32], F32)
    b1sb_b = Buf(f"{tag}_b1sb")
    b2h = ctx.sb(stack, f"{tag}_b2h", [128, D], F32)
    b2h_b = Buf(f"{tag}_b2h")
    psb = ctx.psum(stack, f"{tag}_psb", [128, 512], F32)
    psb_b = Buf(f"{tag}_psb")
    with ctx.scope() as st2:
        b1row = ctx.sb(st2, f"{tag}_b1row", [1, 4096], F32)
        b1row_b = Buf(f"{tag}_b1row")
        b2row = ctx.sb(st2, f"{tag}_b2row", [1, D], F32)
        b2row_b = Buf(f"{tag}_b2row")
        ctx.emit(ctx.sp, lambda e: e.dma_start(out=b1row[:, :], in_=b1[:, :]),
                 writes=[b1row_b], inc=16)
        ctx.emit(ctx.sp, lambda e: e.dma_start(out=b2row[:, :], in_=b2[:, :]),
                 writes=[b2row_b], inc=16)
        for fc in range(32):
            ff = fc
            ctx.mm_group([lambda e, ff=ff: e.matmul(psb[:, 0:1], b1row[0:1, ff * 128:ff * 128 + 128],
                                                    ones_sb[0:1, 0:1], start=True, stop=True)],
                         reads=[b1row_b, ones_b], writes=[psb_b])
            ctx.emit(ctx.dve, lambda e, ff=ff: e.tensor_copy(b1sb[:, ff:ff + 1], psb[:, 0:1]),
                     reads=[psb_b], writes=[b1sb_b])
        _bcast_row_to_partitions(ctx, b2row, b2row_b, ones_sb, ones_b, psb, psb_b,
                                 b2h, b2h_b, D, scale=0.5)

    xlnfm = [ctx.sb(stack, f"{tag}_xlnfm{i}", [128, 8, 512], BF16) for i in range(2)]
    xlnfm_b = [Buf(f"{tag}_xlnfm{i}") for i in range(2)]
    hbf = ctx.sb(stack, f"{tag}_hbf", [128, 32, 512], BF16)
    hbf_b = [Buf(f"{tag}_hbf{fc}") for fc in range(32)]
    xt = [ctx.sb(stack, f"{tag}_xt{i}", [128, D], F32) for i in range(2)]
    xt_b = [Buf(f"{tag}_xt{i}") for i in range(2)]
    xr = xt
    xr_b = xt_b
    xnew = [ctx.sb(stack, f"{tag}_xnew{i}", [128, D], F32) for i in range(2)]
    xnew_b = [Buf(f"{tag}_xnew{i}") for i in range(2)]
    stats = _alloc_ln_slots(ctx, f"{tag}_ln")
    ptr = ctx.psum(stack, f"{tag}_ptr", [128, 8, 128], BF16)
    ptr_b = Buf(f"{tag}_ptr")
    psH = [ctx.psum(stack, f"{tag}_psH{i}", [128, 512], F32) for i in range(2)]
    psH_b = [Buf(f"{tag}_psH{i}") for i in range(2)]
    psO = [ctx.psum(stack, f"{tag}_psO{i}", [128, 512], F32) for i in range(2)]
    psO_b = [Buf(f"{tag}_psO{i}") for i in range(2)]

    ntile = TOK // 512
    s_ln = ctx.stream_sem(f"{tag}_lns")
    for t in range(ntile):
        par = t % 2
        for sub in range(4):
            row0 = t * 512 + sub * 128
            sl = stats[sub % 2]
            ctx.emit(ctx.sp, lambda e, row0=row0, par=par: e.dma_start(
                out=xt[par][:, :], in_=xsrc[row0:row0 + 128, :]),
                writes=[xt_b[par]], inc=16, sem=s_xl)
            _layernorm_subtile(ctx, xt[par], xt_b[par], sl, sub * 128,
                               xlnfm[par], xlnfm_b[par], id_sb, id_b, ptr, ptr_b)
        for k in range(8):
            ctx.emit(ctx.pool, lambda e, k=k, t=t, par=par: e.dma_start(
                out=xln_d[k * 128:k * 128 + 128, t * 512:t * 512 + 512],
                in_=xlnfm[par][:, k, :]),
                reads=[xlnfm_b[par]], inc=16, sem=s_ln)
    for t in range(ntile):
        par = t % 2
        for k in range(8):
            ctx.emit(ctx.sp, lambda e, k=k, t=t, par=par: e.dma_start(
                out=xlnfm[par][:, k, :],
                in_=xln_d[k * 128:k * 128 + 128, t * 512:t * 512 + 512]),
                writes=[xlnfm_b[par]], inc=16, sem=s_ln)
        for fc in range(32):
            ff = fc
            pp = fc % 2
            fns = [
                (lambda e, kk=k, ff=ff, pp=pp, par=par: e.matmul(
                    psH[pp][:, :], wsb1[:, kk, ff * 128:ff * 128 + 128],
                    xlnfm[par][:, kk, :], start=(kk == 0), stop=(kk == 7)))
                for k in range(8)
            ]
            ctx.mm_group(fns, reads=[wsb1_b, xlnfm_b[par]], writes=[psH_b[pp]])
            ctx.emit(ctx.act, lambda e, ff=ff, pp=pp: e.activation(
                hbf[:, ff, :], psH[pp][:, :], AF.Gelu, bias=b1sb[:, ff:ff + 1]),
                reads=[psH_b[pp]], sreads=[b1sb_b], writes=[hbf_b[ff]])
        for sub in range(4):
            row0 = t * 512 + sub * 128
            ctx.emit(ctx.sp, lambda e, row0=row0, par=par: e.dma_start(
                out=xr[par][:, :], in_=xsrc[row0:row0 + 128, :]),
                writes=[xr_b[par]], inc=16, sem=s_xl)
            for half in range(2):
                pp = half
                fns = [
                    (lambda e, ff=fc, sub=sub, half=half, pp=pp, par=par: e.matmul(
                        psO[pp][:, :], hbf[:, ff, sub * 128:sub * 128 + 128],
                        wsb2[:, ff, half * 512:half * 512 + 512],
                        start=(ff == 0), stop=(ff == 31)))
                    for fc in range(32)
                ]
                ctx.mm_group(fns, reads=[wsb2_b] + hbf_b, writes=[psO_b[pp]])
                ctx.emit(ctx.dve, lambda e, half=half, pp=pp, par=par: e.tensor_tensor(
                    xnew[par][:, half * 512:half * 512 + 512], psO[pp][:, :],
                    b2h[:, half * 512:half * 512 + 512], ALU.add),
                    reads=[psO_b[pp], b2h_b], writes=[xnew_b[par]])
            ctx.emit(ctx.dve, lambda e, par=par: e.tensor_tensor(
                xnew[par][:, :], xnew[par][:, :], xr[par][:, :], ALU.add),
                reads=[xr_b[par]], writes=[xnew_b[par]])
            ctx.emit(ctx.pool, lambda e, row0=row0, par=par: e.dma_start(
                out=xdst[row0:row0 + 128, :], in_=xnew[par][:, :]),
                reads=[xnew_b[par]], inc=16, sem=s_st)


def _attn_phase(ctx, xsrc, xdst, wk, wq, wo, bo, id_sb, id_b, ones_sb, ones_b, xln_d):
    nc = ctx.nc
    stack = ctx._stack
    SEQ = 2048

    stats = _alloc_ln_slots(ctx, "a_ln")
    xt = [ctx.sb(stack, f"a_xt{i}", [128, D], F32) for i in range(2)]
    xt_b = [Buf(f"a_xt{i}") for i in range(2)]
    xnew = [ctx.sb(stack, f"a_xnew{i}", [128, D], F32) for i in range(2)]
    xnew_b = [Buf(f"a_xnew{i}") for i in range(2)]
    pso = [ctx.psum(stack, f"a_pso{i}", [65, 512], F32) for i in range(2)]
    pso_b = [Buf(f"a_pso{i}") for i in range(2)]
    prdb = ctx.psum(stack, "a_prdb", [64, 512], F32)
    prdb_b = Buf("a_prdb")
    ebf = [ctx.sb(stack, f"a_ebf{i}", [128, 1024], BF16) for i in range(2)]
    ebf_b = [Buf(f"a_ebf{i}") for i in range(2)]
    osb = [ctx.sb(stack, f"a_osb{i}", [64, 512], F32) for i in range(2)]
    osb_b = [Buf(f"a_osb{i}") for i in range(2)]
    rd = [ctx.sb(stack, f"a_rd{i}", [1, 512], F32) for i in range(2)]
    rd_b = [Buf(f"a_rd{i}") for i in range(2)]
    s_xl = ctx.stream_sem("a_xl")
    s_wl = ctx.stream_sem("a_wl")
    s_st = ctx.stream_sem("a_st")

    for bseq in range(2):
        with ctx.scope() as bstk:
            attn = ctx.sb(bstk, f"a_attn{bseq}", [128, 8, SEQ], BF16)
            attn_b = Buf(f"a_attn{bseq}")
            with ctx.scope() as s2:
                qfm = ctx.sb(s2, f"a_qfm{bseq}", [128, 8, SEQ], BF16)
                qfm_b = Buf(f"a_qfm{bseq}")
                kfm = ctx.sb(s2, f"a_kfm{bseq}", [128, 8, SEQ], BF16)
                kfm_b = Buf(f"a_kfm{bseq}")
                ktm = ctx.sb(s2, f"a_ktm{bseq}", [128, 16, 1040], BF16)
                ktm_b = [Buf(f"a_ktm{i}") for i in range(16)]
                ones_view = bass.AP(ktm[:, :, :].tensor, 64,
                                    [[16 * 1040, 128], [1040, 16], [65, 16], [1, 1]])
                ctx.emit(ctx.dve, lambda e: e.memset(ones_view, 1.0), writes=ktm_b)
                with ctx.scope() as s1:
                    ptr = ctx.psum(s1, f"a_ptr{bseq}", [128, 8, 128], BF16)
                    ptr_b = Buf("a_ptr")
                    psA = [ctx.psum(s1, f"a_psA{bseq}{i}", [128, 512], F32) for i in range(2)]
                    psA_b = [Buf(f"a_psA{i}") for i in range(2)]
                    wqsb = ctx.sb(s1, f"a_wqsb{bseq}", [128, 8, D], BF16)
                    wqsb_b = Buf(f"a_wqsb{bseq}")
                    wksb = ctx.sb(s1, f"a_wksb{bseq}", [128, 8, D], BF16)
                    wksb_b = Buf(f"a_wksb{bseq}")
                    xlnfm = ctx.sb(s1, f"a_xlnfm{bseq}", [128, 8, 512], BF16)
                    xlnfm_b = Buf(f"a_xlnfm{bseq}")
                    ctx.emit(ctx.pool, lambda e: e.dma_start(
                        out=wqsb[:, :, :], in_=wq.ap().rearrange("(c p) n -> p c n", p=128)),
                        writes=[wqsb_b], inc=16, sem=s_wl)
                    ctx.emit(ctx.pool, lambda e: e.dma_start(
                        out=wksb[:, :, :], in_=wk.ap().rearrange("(c p) n -> p c n", p=128)),
                        writes=[wksb_b], inc=16, sem=s_wl)
                    s_ln = ctx.stream_sem(f"a_lns{bseq}")
                    for tg in range(4):
                        for sub in range(4):
                            row0 = bseq * SEQ + tg * 512 + sub * 128
                            par = sub % 2
                            ctx.emit(ctx.sp, lambda e, row0=row0, par=par: e.dma_start(
                                out=xt[par][:, :], in_=xsrc[row0:row0 + 128, :]),
                                writes=[xt_b[par]], inc=16, sem=s_xl)
                            _layernorm_subtile(ctx, xt[par], xt_b[par], stats[sub % 2], sub * 128,
                                               xlnfm, xlnfm_b, id_sb, id_b, ptr, ptr_b)
                        for k in range(8):
                            ctx.emit(ctx.pool, lambda e, k=k, tg=tg: e.dma_start(
                                out=xln_d[k * 128:k * 128 + 128, tg * 512:tg * 512 + 512],
                                in_=xlnfm[:, k, :]),
                                reads=[xlnfm_b], inc=16, sem=s_ln)
                    for tg in range(4):
                        for k in range(8):
                            ctx.emit(ctx.sp, lambda e, k=k, tg=tg: e.dma_start(
                                out=xlnfm[:, k, :],
                                in_=xln_d[k * 128:k * 128 + 128, tg * 512:tg * 512 + 512]),
                                writes=[xlnfm_b], inc=16, sem=s_ln)
                        for (wsb, wsb_b, dstt, dst_b) in ((wqsb, wqsb_b, qfm, qfm_b),
                                                          (wksb, wksb_b, kfm, kfm_b)):
                            for dc in range(8):
                                pp = dc % 2
                                fns = [
                                    (lambda e, kk=k, dc=dc, pp=pp: e.matmul(
                                        psA[pp][:, :], wsb[:, kk, dc * 128:dc * 128 + 128],
                                        xlnfm[:, kk, :], start=(kk == 0), stop=(kk == 7)))
                                    for k in range(8)
                                ]
                                ctx.mm_group(fns, reads=[wsb_b, xlnfm_b], writes=[psA_b[pp]])
                                ctx.emit(ctx.act, lambda e, dc=dc, tg=tg, pp=pp: e.activation(
                                    dstt[:, dc, tg * 512:tg * 512 + 512], psA[pp][:, :], AF.Copy),
                                    reads=[psA_b[pp]], writes=[dst_b])
                    for kt in range(16):
                        fns = [
                            (lambda e, dc=dc, kt=kt: e.transpose(
                                ptr[:, dc, :], kfm[:, dc, kt * 128:kt * 128 + 128], id_sb[:, :]))
                            for dc in range(8)
                        ]
                        ctx.mm_group(fns, reads=[kfm_b, id_b], writes=[ptr_b])
                        for dc in range(8):
                            dst = bass.AP(ktm[:, :, :].tensor, kt * 1040 + dc * 130,
                                          [[16 * 1040, 128], [65, 2], [1, 64]])
                            srcv = ptr[:, dc, :].rearrange("p (a b) -> p a b", b=64)
                            ctx.emit(ctx.dve, lambda e, dst=dst, srcv=srcv: e.tensor_copy(dst, srcv),
                                     reads=[ptr_b], writes=[ktm_b[kt]])
                psS = [ctx.psum(s2, f"a_psS{bseq}{i}", [128, 1024], F32) for i in range(2)]
                psS_b = [Buf(f"a_psS{i}") for i in range(2)]
                for h in range(16):
                    pb = 64 * (h & 1)
                    chq = h // 2
                    for qq in range(4):
                        po = qq % 2
                        for kp in range(8):
                            j = kp % 2
                            t0, t1 = 2 * kp, 2 * kp + 1
                            ctx.mm_group([
                                (lambda e, qq=qq, t0=t0, j=j, pb=pb, chq=chq: e.matmul(
                                    psS[j][:, 0:512],
                                    kfm[pb:pb + 64, chq, t0 * 128:t0 * 128 + 128],
                                    qfm[pb:pb + 64, chq, qq * 512:qq * 512 + 512],
                                    start=True, stop=True)),
                                (lambda e, qq=qq, t1=t1, j=j, pb=pb, chq=chq: e.matmul(
                                    psS[j][:, 512:1024],
                                    kfm[pb:pb + 64, chq, t1 * 128:t1 * 128 + 128],
                                    qfm[pb:pb + 64, chq, qq * 512:qq * 512 + 512],
                                    start=True, stop=True))],
                                reads=[kfm_b, qfm_b], writes=[psS_b[j]])
                            ctx.emit(ctx.act, lambda e, j=j: e.activation(
                                ebf[j][:, :], psS[j][:, :], AF.Exp, scale=1.0 / 32.0),
                                reads=[psS_b[j]], writes=[ebf_b[j]])
                            ctx.mm_group([
                                (lambda e, h=h, t0=t0, j=j, kp=kp, po=po: e.matmul(
                                    pso[po][:, :], ktm[:, t0, 65 * h:65 * h + 65],
                                    ebf[j][:, 0:512], start=(kp == 0), stop=False,
                                    skip_group_check=True)),
                                (lambda e, h=h, t1=t1, j=j, kp=kp, po=po: e.matmul(
                                    pso[po][:, :], ktm[:, t1, 65 * h:65 * h + 65],
                                    ebf[j][:, 512:1024], start=False, stop=(kp == 7),
                                    skip_group_check=True))],
                                reads=[ktm_b[t0], ktm_b[t1], ebf_b[j]], writes=[pso_b[po]])
                        ctx.emit(ctx.dve, lambda e, po=po: e.reciprocal(
                            rd[po][:, :], pso[po][64:65, :]),
                            reads=[pso_b[po]], writes=[rd_b[po]])
                        ctx.mm_group([lambda e, po=po: e.matmul(
                            prdb[:, :], ones_sb[0:1, 0:64], rd[po][0:1, :],
                            start=True, stop=True)],
                            reads=[rd_b[po], ones_b], writes=[prdb_b])
                        ctx.emit(ctx.act, lambda e, po=po: e.activation(
                            osb[po][:, :], pso[po][0:64, :], AF.Copy),
                            reads=[pso_b[po]], writes=[osb_b[po]])
                        ctx.emit(ctx.dve, lambda e, h=h, qq=qq, po=po, pb=pb, chq=chq:
                                 e.tensor_tensor(
                                     attn[pb:pb + 64, chq, qq * 512:qq * 512 + 512],
                                     osb[po][:, :], prdb[:, :], ALU.mult),
                                 reads=[osb_b[po], prdb_b], writes=[attn_b])
            with ctx.scope() as s3:
                psE = [ctx.psum(s3, f"a_psE{bseq}{i}", [128, 512], F32) for i in range(2)]
                psE_b = [Buf(f"a_psE{i}") for i in range(2)]
                wosb = ctx.sb(s3, f"a_wosb{bseq}", [128, 8, D], BF16)
                wosb_b = Buf(f"a_wosb{bseq}")
                bobc = ctx.sb(s3, f"a_bobc{bseq}", [128, D], F32)
                bobc_b = Buf(f"a_bobc{bseq}")
                ctx.emit(ctx.pool, lambda e: e.dma_start(
                    out=wosb[:, :, :], in_=wo.ap().rearrange("(c p) n -> p c n", p=128)),
                    writes=[wosb_b], inc=16, sem=s_wl)
                with ctx.scope() as s4:
                    borow = ctx.sb(s4, f"a_borow{bseq}", [1, D], F32)
                    borow_b = Buf(f"a_borow{bseq}")
                    ctx.emit(ctx.sp, lambda e: e.dma_start(out=borow[:, :], in_=bo[:, :]),
                             writes=[borow_b], inc=16)
                    _bcast_row_to_partitions(ctx, borow, borow_b, ones_sb, ones_b,
                                             psE[0], psE_b[0], bobc, bobc_b, D)
                for sub in range(16):
                    row0 = bseq * SEQ + sub * 128
                    par = sub % 2
                    ctx.emit(ctx.sp, lambda e, row0=row0, par=par: e.dma_start(
                        out=xt[par][:, :], in_=xsrc[row0:row0 + 128, :]),
                        writes=[xt_b[par]], inc=16, sem=s_xl)
                    for half in range(2):
                        pp = half
                        fns = [
                            (lambda e, c=c, sub=sub, half=half, pp=pp: e.matmul(
                                psE[pp][:, :], attn[:, c, sub * 128:sub * 128 + 128],
                                wosb[:, c, half * 512:half * 512 + 512],
                                start=(c == 0), stop=(c == 7)))
                            for c in range(8)
                        ]
                        ctx.mm_group(fns, reads=[attn_b, wosb_b], writes=[psE_b[pp]])
                        ctx.emit(ctx.dve, lambda e, half=half, pp=pp, par=par: e.tensor_tensor(
                            xnew[par][:, half * 512:half * 512 + 512], psE[pp][:, :],
                            bobc[:, half * 512:half * 512 + 512], ALU.add),
                            reads=[psE_b[pp], bobc_b], writes=[xnew_b[par]])
                    ctx.emit(ctx.dve, lambda e, par=par: e.tensor_tensor(
                        xnew[par][:, :], xnew[par][:, :], xt[par][:, :], ALU.add),
                        reads=[xt_b[par]], writes=[xnew_b[par]])
                    ctx.emit(ctx.pool, lambda e, row0=row0, par=par: e.dma_start(
                        out=xdst[row0:row0 + 128, :], in_=xnew[par][:, :]),
                        reads=[xnew_b[par]], inc=16, sem=s_st)


def build_launch_B():
    nc = bass.Bass("TRN2", target_bir_lowering=False, debug=False)

    xin = nc.dram_tensor("xin", [TOK, D], F32, kind="ExternalInput")
    wgm = nc.dram_tensor("wgm", [D, 8], F32, kind="ExternalInput")
    w1m = nc.dram_tensor("w1m", [8 * 8 * D, 256], F32, kind="ExternalInput")
    w2m = nc.dram_tensor("w2m", [8 * 8 * D, 256], F32, kind="ExternalInput")
    w3m = nc.dram_tensor("w3m", [8 * 2048, D], F32, kind="ExternalInput")
    convw = nc.dram_tensor("convw", [D, 4], F32, kind="ExternalInput")
    wq1 = nc.dram_tensor("wq1", [D, 512], F32, kind="ExternalInput")
    wkg = nc.dram_tensor("wkg", [D, 512], F32, kind="ExternalInput")
    wv1 = nc.dram_tensor("wv1", [D, D], F32, kind="ExternalInput")
    wg1 = nc.dram_tensor("wg1", [D, D], F32, kind="ExternalInput")
    bg1 = nc.dram_tensor("bg1", [1, D], F32, kind="ExternalInput")
    wout = nc.dram_tensor("wout", [D, D], F32, kind="ExternalInput")
    augb = nc.dram_tensor("augb", [1, 512], F32, kind="ExternalInput")
    ident = nc.dram_tensor("ident", [128, 128], BF16, kind="ExternalInput")
    onesf = nc.dram_tensor("onesf", [128, 128], F32, kind="ExternalInput")
    onesb = nc.dram_tensor("onesb", [128, 128], BF16, kind="ExternalInput")
    identf = nc.dram_tensor("identf", [128, 128], F32, kind="ExternalInput")
    eqk = nc.dram_tensor("eqk", [2 * 128, 8], BF16, kind="ExternalInput")
    ebcA = nc.dram_tensor("ebcA", [4 * 8, 128], BF16, kind="ExternalInput")
    esumV = nc.dram_tensor("esumV", [4 * 128, 8], F32, kind="ExternalInput")
    ebcVf = nc.dram_tensor("ebcVf", [4 * 8, 128], F32, kind="ExternalInput")

    xout = nc.dram_tensor("xout", [TOK, D], F32, kind="ExternalOutput")

    xs1 = nc.dram_tensor("xs1", [TOK, D], F32)
    xs2 = nc.dram_tensor("xs2", [TOK, D], F32)
    xcfm_d = nc.dram_tensor("xcfm_d", [D, TOK], BF16)
    gsil_d = nc.dram_tensor("gsil_d", [D, TOK], BF16)

    ctx = Ctx(nc)
    with nc.Block():
        id_sb = nc.alloc_sbuf_tensor("id_sb", [128, 128], BF16)
        ones_sb = nc.alloc_sbuf_tensor("ones_sb", [128, 128], F32)
        id_b, ones_b = Buf("id"), Buf("ones")
        ctx.emit(ctx.sp, lambda e: e.dma_start(out=id_sb[:, :], in_=ident[:, :]),
                 writes=[id_b], inc=16)
        ctx.emit(ctx.sp, lambda e: e.dma_start(out=ones_sb[:, :], in_=onesf[:, :]),
                 writes=[ones_b], inc=16)
        onesbf_sb = nc.alloc_sbuf_tensor("onesbf_sb", [128, 128], BF16)
        onesbf_b = Buf("onesbf")
        ctx.emit(ctx.sp, lambda e: e.dma_start(out=onesbf_sb[:, :], in_=onesb[:, :]),
                 writes=[onesbf_b], inc=16)
        idf_sb = nc.alloc_sbuf_tensor("idf_sb", [128, 128], F32)
        idf_b = Buf("idf")
        ctx.emit(ctx.sp, lambda e: e.dma_start(out=idf_sb[:, :], in_=identf[:, :]),
                 writes=[idf_b], inc=16)
        cst = dict(id_sb=id_sb, id_b=id_b, ones_sb=ones_sb, ones_b=ones_b,
                   onesbf_sb=onesbf_sb, onesbf_b=onesbf_b, idf_sb=idf_sb, idf_b=idf_b)

        if "p4" in PHASES_B:
            dst = xs1 if ("p5" in PHASES_B or "p6" in PHASES_B) else xout
            with ctx.scope():
                _moe_phase(ctx, xin, dst, wgm, w1m, w2m, w3m, cst, tag="m1")
        if "p5" in PHASES_B:
            src = xs1 if "p4" in PHASES_B else xin
            dst = xs2 if "p6" in PHASES_B else xout
            with ctx.scope():
                _gla_phase(ctx, src, dst, convw, wq1, wkg, wv1, wg1, bg1, wout, augb,
                           eqk, ebcA, esumV, ebcVf, xcfm_d, gsil_d, cst)
        if "p6" in PHASES_B:
            src = xs2 if "p5" in PHASES_B else (xs1 if "p4" in PHASES_B else xin)
            with ctx.scope():
                _moe_phase(ctx, src, xout, wgm, w1m, w2m, w3m, cst, tag="m2")
        ctx.barrier_all()
    return nc


def _moe_phase(ctx, xsrc, xdst, wgm, w1m, w2m, w3m, cst, tag):
    nc = ctx.nc
    stack = ctx._stack
    id_sb, id_b = cst["id_sb"], cst["id_b"]
    ones_sb, ones_b = cst["ones_sb"], cst["ones_b"]
    onesbf_sb, onesbf_b = cst["onesbf_sb"], cst["onesbf_b"]

    s_wl = ctx.stream_sem(f"{tag}_wl")
    s_st = ctx.stream_sem(f"{tag}_st")
    s_xl = ctx.stream_sem(f"{tag}_xl")
    xlnfm = ctx.sb(stack, f"{tag}_xlnfm", [128, 8, TOK], BF16)
    xlnfm_bs = [Buf(f"{tag}_xlnfm{g}") for g in range(NSUB // 4)]
    gv = ctx.sb(stack, f"{tag}_gv", [128, 32, 8], F32)
    gv_b = Buf(f"{tag}_gv")

    xdst_b = [Buf(f"{tag}_xdst{i}") for i in range(32)]

    s_wl3 = ctx.stream_sem(f"{tag}_wl3")
    w1sb = ctx.sb(stack, f"{tag}_w1sb", [128, 8, 2048], BF16)
    w1sb_b = [Buf(f"{tag}_w1sb{c}") for c in range(8)]
    w2sb = ctx.sb(stack, f"{tag}_w2sb", [128, 8, 2048], BF16)
    w2sb_b = [Buf(f"{tag}_w2sb{c}") for c in range(8)]
    w3sb = ctx.sb(stack, f"{tag}_w3sb", [128, 16, D], BF16)
    w3sb_b = [Buf(f"{tag}_w3sb{h}") for h in range(2)]

    def _load_w12_chunk(ex, c8):
        for (wsb, wsb_b, wm) in ((w1sb, w1sb_b, w1m), (w2sb, w2sb_b, w2m)):
            r0 = (ex * 8 + c8) * D
            ctx.emit(ctx.pool, lambda e, wsb=wsb, wm=wm, r0=r0, c8=c8: e.dma_start(
                out=wsb[:, :, c8 * 256:c8 * 256 + 256],
                in_=wm[r0:r0 + D, :].rearrange("(c p) n -> p c n", p=128)),
                writes=[wsb_b[c8]], inc=16, sem=s_wl)

    def _load_w3(ex):
        for hh in range(2):
            ctx.emit(ctx.pool, lambda e, ex=ex, hh=hh: e.dma_start(
                out=w3sb[:, hh * 8:hh * 8 + 8, :],
                in_=w3m[ex * 2048 + hh * 1024:ex * 2048 + hh * 1024 + 1024, :]
                    .rearrange("(c p) n -> p c n", p=128)),
                writes=[w3sb_b[hh]], inc=16, sem=s_wl3)

    with ctx.scope() as s1:
        stats = _alloc_ln_slots(ctx, f"{tag}_ln")
        xt = [ctx.sb(s1, f"{tag}_xt{i}", [128, D], F32) for i in range(2)]
        xt_b = [Buf(f"{tag}_xt{i}") for i in range(2)]
        ptr = ctx.psum(s1, f"{tag}_ptr", [128, 8, 128], BF16)
        ptr_b = Buf(f"{tag}_ptr")
        wgsb = ctx.sb(s1, f"{tag}_wgsb", [128, 8, 8], BF16)
        wgsb_b = Buf(f"{tag}_wgsb")
        ctx.emit(ctx.pool, lambda e: e.dma_start(
            out=wgsb[:, :, :], in_=wgm.ap().rearrange("(c p) n -> p c n", p=128)),
            writes=[wgsb_b], inc=16)
        for c8 in range(8):
            _load_w12_chunk(0, c8)
        _load_w3(0)
        gs = ctx.sb(s1, f"{tag}_gs", [128, 32, 8], F32)
        gs_b = Buf(f"{tag}_gs")
        psG = ctx.psum(s1, f"{tag}_psG", [128, 8], F32)
        psG_b = Buf(f"{tag}_psG")
        for sub in range(NSUB):
            r0 = sub * 128
            par = sub % 2
            ctx.emit(ctx.sp, lambda e, r0=r0, par=par: e.dma_start(
                out=xt[par][:, :], in_=xsrc[r0:r0 + 128, :]),
                writes=[xt_b[par]], inc=16)
            _layernorm_subtile(ctx, xt[par], xt_b[par], stats[sub % 2], sub * 128,
                               xlnfm, xlnfm_bs[sub // 4], id_sb, id_b, ptr, ptr_b)
        for sub in range(NSUB):
            fns = [
                (lambda e, kk=k, sub=sub: e.matmul(
                    psG[:, :], xlnfm[:, kk, sub * 128:sub * 128 + 128],
                    wgsb[:, kk, :], start=(kk == 0), stop=(kk == 7)))
                for k in range(8)
            ]
            ctx.mm_group(fns, reads=[xlnfm_bs[sub // 4], wgsb_b], writes=[psG_b])
            ctx.emit(ctx.act, lambda e, sub=sub: e.activation(
                gs[:, sub, :], psG[:, :], AF.Copy),
                reads=[psG_b], writes=[gs_b])
        m1 = ctx.sb(s1, f"{tag}_m1", [128, 8, 1], F32)
        m1_b = Buf(f"{tag}_m1")
        m2 = ctx.sb(s1, f"{tag}_m2", [128, 8, 1], F32)
        m2_b = Buf(f"{tag}_m2")
        mk1 = ctx.sb(s1, f"{tag}_mk1", [128, 8, 8], F32)
        mk1_b = Buf(f"{tag}_mk1")
        mk2 = ctx.sb(s1, f"{tag}_mk2", [128, 8, 8], F32)
        mk2_b = Buf(f"{tag}_mk2")
        gs2 = ctx.sb(s1, f"{tag}_gs2", [128, 8, 8], F32)
        gs2_b = Buf(f"{tag}_gs2")
        dd = ctx.sb(s1, f"{tag}_dd", [128, 8, 1], F32)
        dd_b = Buf(f"{tag}_dd")
        p2v = ctx.sb(s1, f"{tag}_p2", [128, 8, 1], F32)
        p2v_b = Buf(f"{tag}_p2")
        p1h = ctx.sb(s1, f"{tag}_p1h", [128, 8, 1], F32)
        p1h_b = Buf(f"{tag}_p1h")
        p2h = ctx.sb(s1, f"{tag}_p2h", [128, 8, 1], F32)
        p2h_b = Buf(f"{tag}_p2h")
        def bview(t):
            a = t[:, :, 0:1]
            return bass.AP(a.tensor, a.offset, [list(q) for q in a.ap[:-1]] + [[0, 8]])

        for g8 in range(4):
            sl = gs[:, g8 * 8:(g8 + 1) * 8, :]
            ctx.emit(ctx.dve, lambda e, sl=sl: e.reduce_max(m1[:, :, :], sl, axis=AX.X),
                     reads=[gs_b], writes=[m1_b])
            ctx.emit(ctx.dve, lambda e, sl=sl: e.tensor_tensor(
                mk1[:, :, :], sl, bview(m1), ALU.is_ge),
                reads=[gs_b, m1_b], writes=[mk1_b])
            ctx.emit(ctx.dve, lambda e, sl=sl: e.scalar_tensor_tensor(
                gs2[:, :, :], mk1[:, :, :], -1e9, sl, ALU.mult, ALU.add),
                reads=[mk1_b, gs_b], writes=[gs2_b])
            ctx.emit(ctx.dve, lambda e: e.reduce_max(m2[:, :, :], gs2[:, :, :], axis=AX.X),
                     reads=[gs2_b], writes=[m2_b])
            ctx.emit(ctx.dve, lambda e: e.tensor_tensor(
                mk2[:, :, :], gs2[:, :, :], bview(m2), ALU.is_ge),
                reads=[gs2_b, m2_b], writes=[mk2_b])
            ctx.emit(ctx.dve, lambda e: e.tensor_tensor(
                dd[:, :, :], m2[:, :, :], m1[:, :, :], ALU.subtract),
                reads=[m2_b, m1_b], writes=[dd_b])
            ctx.emit(ctx.act, lambda e: e.activation(p2v[:, :, :], dd[:, :, :], AF.Sigmoid),
                     reads=[dd_b], writes=[p2v_b])
            ctx.emit(ctx.dve, lambda e: e.tensor_scalar(
                p1h[:, :, :], p2v[:, :, :], -0.5, 0.5, ALU.mult, ALU.add),
                reads=[p2v_b], writes=[p1h_b])
            ctx.emit(ctx.dve, lambda e: e.tensor_scalar(
                p2h[:, :, :], p2v[:, :, :], 0.5, None, ALU.mult),
                reads=[p2v_b], writes=[p2h_b])
            gvs = gv[:, g8 * 8:(g8 + 1) * 8, :]
            ctx.emit(ctx.dve, lambda e, gvs=gvs: e.tensor_tensor(
                gvs, mk1[:, :, :], bview(p1h), ALU.mult),
                reads=[mk1_b, p1h_b], writes=[gv_b])
            ctx.emit(ctx.dve, lambda e: e.tensor_tensor(
                mk2[:, :, :], mk2[:, :, :], bview(p2h), ALU.mult),
                reads=[p2h_b], writes=[mk2_b])
            ctx.emit(ctx.dve, lambda e, gvs=gvs: e.tensor_tensor(
                gvs, gvs, mk2[:, :, :], ALU.add),
                reads=[mk2_b], writes=[gv_b])

    hbf = ctx.sb(stack, f"{tag}_hbf", [128, 16, 512], BF16)
    hbf_b = [Buf(f"{tag}_hbf{fc}") for fc in range(16)]
    s1f = [ctx.sb(stack, f"{tag}_s1f{i}", [128, 512], F32) for i in range(2)]
    s1f_b = [Buf(f"{tag}_s1f{i}") for i in range(2)]
    obuf = [ctx.sb(stack, f"{tag}_obuf{i}", [128, D], F32) for i in range(2)]
    obuf_b = [Buf(f"{tag}_obuf{i}") for i in range(2)]
    xcur = [ctx.sb(stack, f"{tag}_xcur{i}", [128, D], F32) for i in range(2)]
    xcur_b = [Buf(f"{tag}_xcur{i}") for i in range(2)]
    psH1 = [ctx.psum(stack, f"{tag}_psH1{i}", [128, 512], F32) for i in range(3)]
    psH1_b = [Buf(f"{tag}_psH1{i}") for i in range(3)]
    psH2 = [ctx.psum(stack, f"{tag}_psH2{i}", [128, 512], F32) for i in range(3)]
    psH2_b = [Buf(f"{tag}_psH2{i}") for i in range(3)]
    psO = [ctx.psum(stack, f"{tag}_psO{i}", [128, 512], F32) for i in range(2)]
    psO_b = [Buf(f"{tag}_psO{i}") for i in range(2)]

    for ex in range(MOE_EXPERTS):
        for ts in range(8):
            c0 = ts * 512
            for fc in range(16):
                pp = fc % 3
                fns = [
                    (lambda e, kk=k, fc=fc, pp=pp, c0=c0: e.matmul(
                        psH1[pp][:, :], w1sb[:, kk, fc * 128:fc * 128 + 128],
                        xlnfm[:, kk, c0:c0 + 512], start=(kk == 0), stop=(kk == 7)))
                    for k in range(8)
                ]
                ctx.mm_group(fns, reads=[w1sb_b[fc // 2], xlnfm_bs[ts]], writes=[psH1_b[pp]])
                fns = [
                    (lambda e, kk=k, fc=fc, pp=pp, c0=c0: e.matmul(
                        psH2[pp][:, :], w2sb[:, kk, fc * 128:fc * 128 + 128],
                        xlnfm[:, kk, c0:c0 + 512], start=(kk == 0), stop=(kk == 7)))
                    for k in range(8)
                ]
                ctx.mm_group(fns, reads=[w2sb_b[fc // 2], xlnfm_bs[ts]], writes=[psH2_b[pp]])
                if ts == 7 and fc % 2 == 1 and ex + 1 < MOE_EXPERTS:
                    _load_w12_chunk(ex + 1, fc // 2)
                sp2 = fc % 2
                ctx.emit(ctx.act, lambda e, pp=pp, sp2=sp2: e.activation(
                    s1f[sp2][:, :], psH1[pp][:, :], AF.Silu),
                    reads=[psH1_b[pp]], writes=[s1f_b[sp2]])
                ctx.emit(ctx.dve, lambda e, pp=pp, fc=fc, sp2=sp2: e.tensor_tensor(
                    hbf[:, fc, :], s1f[sp2][:, :], psH2[pp][:, :], ALU.mult),
                    reads=[s1f_b[sp2], psH2_b[pp]], writes=[hbf_b[fc]])
            for sub4 in range(4):
                sub = ts * 4 + sub4
                par = sub % 2
                r0 = sub * 128
                acc_src = xsrc if ex == 0 else xdst
                ctx.emit(ctx.sp, lambda e, r0=r0, par=par, acc_src=acc_src: e.dma_start(
                    out=xcur[par][:, :], in_=acc_src[r0:r0 + 128, :]),
                    reads=[xdst_b[sub]] if ex > 0 else [],
                    writes=[xcur_b[par]], inc=16, sem=s_xl)
                for dh in range(2):
                    pp = dh
                    fns = [
                        (lambda e, fc=fc, sub4=sub4, dh=dh, pp=pp: e.matmul(
                            psO[pp][:, :], hbf[:, fc, sub4 * 128:sub4 * 128 + 128],
                            w3sb[:, fc, dh * 512:dh * 512 + 512],
                            start=(fc == 0), stop=(fc == 15)))
                        for fc in range(16)
                    ]
                    ctx.mm_group(fns, reads=hbf_b + w3sb_b, writes=[psO_b[pp]])
                    ctx.emit(ctx.act, lambda e, dh=dh, pp=pp, par=par, ex=ex, sub=sub:
                             e.activation(obuf[par][:, dh * 512:dh * 512 + 512],
                                          psO[pp][:, :], AF.Identity,
                                          scale=gv[:, sub, ex:ex + 1]),
                             reads=[psO_b[pp]], sreads=[gv_b], writes=[obuf_b[par]])
                ctx.emit(ctx.dve, lambda e, par=par: e.tensor_tensor(
                    obuf[par][:, :], obuf[par][:, :], xcur[par][:, :], ALU.add),
                    reads=[xcur_b[par]], writes=[obuf_b[par]])
                ctx.emit(ctx.pool, lambda e, r0=r0, par=par: e.dma_start(
                    out=xdst[r0:r0 + 128, :], in_=obuf[par][:, :]),
                    reads=[obuf_b[par]], writes=[xdst_b[sub]], inc=16, sem=s_st)
        if ex + 1 < MOE_EXPERTS:
            _load_w3(ex + 1)


def _gla_phase(ctx, xsrc, xdst, convw, wq1, wkg, wv1, wg1, bg1, wout, augb,
               eqk, ebcA, esumV, ebcVf, xcfm_d, gsil_d, cst):
    nc = ctx.nc
    stack = ctx._stack
    id_sb, id_b = cst["id_sb"], cst["id_b"]
    ones_sb, ones_b = cst["ones_sb"], cst["ones_b"]
    RSQ32 = 0.17677669529663687
    s_xl = ctx.stream_sem("g_xl")
    s_wl = ctx.stream_sem("g_wl")
    s_st = ctx.stream_sem("g_st")
    s_sc = ctx.stream_sem("g_sc")

    on_d = nc.dram_tensor("gla_on_d", [D, TOK], BF16)

    eqk_sb = ctx.sb(stack, "g_eqk", [128, 2, 8], BF16)
    eqk_b = Buf("g_eqk")
    ctx.emit(ctx.sp, lambda e: e.dma_start(
        out=eqk_sb[:, :, :], in_=eqk.ap().rearrange("(c p) n -> p c n", p=128)),
        writes=[eqk_b], inc=16)
    ebcA_sb = ctx.sb(stack, "g_ebcA", [8, 4, 128], BF16)
    ebcA_b = Buf("g_ebcA")
    ctx.emit(ctx.sp, lambda e: e.dma_start(
        out=ebcA_sb[:, :, :], in_=ebcA.ap().rearrange("(c p) n -> p c n", p=8)),
        writes=[ebcA_b], inc=16)
    esumV_sb = ctx.sb(stack, "g_esumV", [128, 4, 8], F32)
    esumV_b = Buf("g_esumV")
    ctx.emit(ctx.sp, lambda e: e.dma_start(
        out=esumV_sb[:, :, :], in_=esumV.ap().rearrange("(c p) n -> p c n", p=128)),
        writes=[esumV_b], inc=16)
    ebcVf_sb = ctx.sb(stack, "g_ebcVf", [8, 4, 128], F32)
    ebcVf_b = Buf("g_ebcVf")
    ctx.emit(ctx.sp, lambda e: e.dma_start(
        out=ebcVf_sb[:, :, :], in_=ebcVf.ap().rearrange("(c p) n -> p c n", p=8)),
        writes=[ebcVf_b], inc=16)

    with ctx.scope() as s0:
        stats = _alloc_ln_slots(ctx, "g_ln")
        xt = [ctx.sb(s0, f"g_xt{i}", [128, D], F32) for i in range(2)]
        xt_b = [Buf(f"g_xt{i}") for i in range(2)]
        ptr = ctx.psum(s0, "g_ptr", [128, 8, 128], BF16)
        ptr_b = Buf("g_ptr")
        xlnfm = ctx.sb(s0, "g_xlnfm", [128, 8, TOK], BF16)
        xlnfm_b = Buf("g_xlnfm")
        for sub in range(NSUB):
            r0 = sub * 128
            par = sub % 2
            ctx.emit(ctx.sp, lambda e, r0=r0, par=par: e.dma_start(
                out=xt[par][:, :], in_=xsrc[r0:r0 + 128, :]),
                writes=[xt_b[par]], inc=16, sem=s_xl)
            _layernorm_subtile(ctx, xt[par], xt_b[par], stats[sub % 2], sub * 128,
                               xlnfm, xlnfm_b, id_sb, id_b, ptr, ptr_b)
        convsb = ctx.sb(s0, "g_convsb", [128, 8, 4], F32)
        convsb_b = Buf("g_convsb")
        ctx.emit(ctx.sp, lambda e: e.dma_start(
            out=convsb[:, :, :], in_=convw.ap().rearrange("(c p) n -> p c n", p=128)),
            writes=[convsb_b], inc=16)
        yc = ctx.sb(s0, "g_yc", [128, TOK], F32)
        yc_b = Buf("g_yc")
        xcbf = ctx.sb(s0, "g_xcbf", [128, TOK], BF16)
        xcbf_b = Buf("g_xcbf")
        for c in range(8):
            ctx.emit(ctx.act, lambda e, c=c: e.activation(
                yc[:, :], xlnfm[:, c, :], AF.Identity, scale=convsb[:, c, 3:4]),
                reads=[xlnfm_b], sreads=[convsb_b], writes=[yc_b])
            for j in range(1, 4):
                sh = 256 * j
                ctx.emit(ctx.dve, lambda e, c=c, j=j, sh=sh: e.scalar_tensor_tensor(
                    yc[:, sh:TOK], xlnfm[:, c, 0:TOK - sh], convsb[:, c, 3 - j:4 - j],
                    yc[:, sh:TOK], ALU.mult, ALU.add),
                    reads=[xlnfm_b], sreads=[convsb_b], writes=[yc_b])
            ctx.emit(ctx.act, lambda e, c=c: e.activation(xcbf[:, :], yc[:, :], AF.Silu),
                     reads=[yc_b], writes=[xcbf_b])
            ctx.emit(ctx.pool, lambda e, c=c: e.dma_start(
                out=xcfm_d[c * 128:(c + 1) * 128, :], in_=xcbf[:, :]),
                reads=[xcbf_b], inc=16, sem=s_st)

    with ctx.scope() as s2:
        wgsb = ctx.sb(s2, "g_wgsb", [128, 8, D], BF16)
        wgsb_b = Buf("g_wgsb")
        ctx.emit(ctx.pool, lambda e: e.dma_start(
            out=wgsb[:, :, :], in_=wg1.ap().rearrange("(c p) n -> p c n", p=128)),
            writes=[wgsb_b], inc=16, sem=s_wl)
        bgsb = ctx.sb(s2, "g_bgsb", [128, 8], F32)
        bgsb_b = Buf("g_bgsb")
        bgrow = ctx.sb(s2, "g_bgrow", [1, D], F32)
        bgrow_b = Buf("g_bgrow")
        ctx.emit(ctx.sp, lambda e: e.dma_start(out=bgrow[:, :], in_=bg1[:, :]),
                 writes=[bgrow_b], inc=16)
        psA = [ctx.psum(s2, f"g_psA{i}", [128, 512], F32) for i in range(2)]
        psA_b = [Buf(f"g_psA{i}") for i in range(2)]
        for dc in range(8):
            ctx.mm_group([lambda e, dc=dc: e.matmul(
                psA[0][:, 0:1], bgrow[0:1, dc * 128:dc * 128 + 128],
                ones_sb[0:1, 0:1], start=True, stop=True)],
                reads=[bgrow_b, ones_b], writes=[psA_b[0]])
            ctx.emit(ctx.dve, lambda e, dc=dc: e.tensor_copy(bgsb[:, dc:dc + 1], psA[0][:, 0:1]),
                     reads=[psA_b[0]], writes=[bgsb_b])
        xct = [ctx.sb(s2, f"g_xct{i}", [128, 8, 512], BF16) for i in range(2)]
        xct_b = [Buf(f"g_xct{i}") for i in range(2)]
        gt = ctx.sb(s2, "g_gt", [128, 512], BF16)
        gt_b = Buf("g_gt")
        for tr in range(8):
            par = tr % 2
            for k in range(8):
                ctx.emit(ctx.sp, lambda e, k=k, tr=tr, par=par: e.dma_start(
                    out=xct[par][:, k, :], in_=xcfm_d[k * 128:k * 128 + 128,
                                                      tr * 512:tr * 512 + 512]),
                    writes=[xct_b[par]], inc=16, sem=s_sc)
            for dc in range(8):
                pp = dc % 2
                fns = [
                    (lambda e, k=k, dc=dc, pp=pp, par=par: e.matmul(
                        psA[pp][:, :], wgsb[:, k, dc * 128:dc * 128 + 128],
                        xct[par][:, k, :], start=(k == 0), stop=(k == 7)))
                    for k in range(8)
                ]
                ctx.mm_group(fns, reads=[wgsb_b, xct_b[par]], writes=[psA_b[pp]])
                ctx.emit(ctx.act, lambda e, dc=dc, pp=pp: e.activation(
                    gt[:, :], psA[pp][:, :], AF.Silu, bias=bgsb[:, dc:dc + 1]),
                    reads=[psA_b[pp]], sreads=[bgsb_b], writes=[gt_b])
                ctx.emit(ctx.pool, lambda e, dc=dc, tr=tr: e.dma_start(
                    out=gsil_d[dc * 128:dc * 128 + 128, tr * 512:tr * 512 + 512],
                    in_=gt[:, :]),
                    reads=[gt_b], inc=16, sem=s_st)

    for hg in range(2):
        with ctx.scope() as sg:
            qsb = ctx.sb(sg, f"g_qsb{hg}", [128, 2, TOK], BF16)
            qsb_b = Buf(f"g_qsb{hg}")
            ksb = ctx.sb(sg, f"g_ksb{hg}", [128, 2, TOK], BF16)
            ksb_b = Buf(f"g_ksb{hg}")
            vsb = ctx.sb(sg, f"g_vsb{hg}", [128, 4, TOK], BF16)
            vsb_b = Buf(f"g_vsb{hg}")
            augw = ctx.sb(sg, f"g_augw{hg}", [8, TOK], BF16)
            augw_b = Buf(f"g_augw{hg}")
            gkraw = ctx.sb(sg, f"g_gkraw{hg}", [128, 2, TOK], F32)
            gkraw_b = [Buf(f"g_gkraw{hg}_{c}") for c in range(2)]
            psA8 = ctx.psum(sg, f"g_psA8{hg}", [8, 512], F32)
            psA8_b = Buf(f"g_psA8{hg}")

            with ctx.scope() as sa:
                wq1sb = ctx.sb(sa, f"g_wq1sb{hg}", [128, 8, 256], BF16)
                wq1sb_b = Buf("g_wq1sb")
                ctx.emit(ctx.pool, lambda e, hg=hg: e.dma_start(
                    out=wq1sb[:, :, :],
                    in_=wq1[:, 256 * hg:256 * hg + 256].rearrange("(c p) n -> p c n", p=128)),
                    writes=[wq1sb_b], inc=16)
                wkgsb = ctx.sb(sa, f"g_wkgsb{hg}", [128, 8, 256], BF16)
                wkgsb_b = Buf("g_wkgsb")
                ctx.emit(ctx.pool, lambda e, hg=hg: e.dma_start(
                    out=wkgsb[:, :, :],
                    in_=wkg[:, 256 * hg:256 * hg + 256].rearrange("(c p) n -> p c n", p=128)),
                    writes=[wkgsb_b], inc=16)
                wvsb = ctx.sb(sa, f"g_wvsb{hg}", [128, 8, 512], BF16)
                wvsb_b = Buf("g_wvsb")
                ctx.emit(ctx.pool, lambda e, hg=hg: e.dma_start(
                    out=wvsb[:, :, :],
                    in_=wv1[:, 512 * hg:512 * hg + 512].rearrange("(c p) n -> p c n", p=128)),
                    writes=[wvsb_b], inc=16)
                xct2 = [ctx.sb(sa, f"g_xct2{hg}{i}", [128, 8, 512], BF16) for i in range(2)]
                xct2_b = [Buf(f"g_xct2{hg}{i}") for i in range(2)]
                psP = [ctx.psum(sa, f"g_psP{hg}{i}", [128, 512], F32) for i in range(2)]
                psP_b = [Buf(f"g_psP{hg}{i}") for i in range(2)]
                for tr in range(8):
                    par = tr % 2
                    for k in range(8):
                        ctx.emit(ctx.sp, lambda e, k=k, tr=tr, par=par: e.dma_start(
                            out=xct2[par][:, k, :], in_=xcfm_d[k * 128:k * 128 + 128,
                                                              tr * 512:tr * 512 + 512]),
                            writes=[xct2_b[par]], inc=16, sem=s_sc)
                    for cq in range(2):
                        pp = cq
                        fns = [
                            (lambda e, k=k, cq=cq, pp=pp, par=par: e.matmul(
                                psP[pp][:, :], wq1sb[:, k, cq * 128:cq * 128 + 128],
                                xct2[par][:, k, :], start=(k == 0), stop=(k == 7)))
                            for k in range(8)
                        ]
                        ctx.mm_group(fns, reads=[wq1sb_b, xct2_b[par]], writes=[psP_b[pp]])
                        ctx.emit(ctx.act, lambda e, cq=cq, tr=tr, pp=pp: e.activation(
                            qsb[:, cq, tr * 512:tr * 512 + 512], psP[pp][:, :],
                            AF.Identity, scale=RSQ32),
                            reads=[psP_b[pp]], writes=[qsb_b])
                    for cq in range(2):
                        pp = cq
                        fns = [
                            (lambda e, k=k, cq=cq, pp=pp, par=par: e.matmul(
                                psP[pp][:, :], wkgsb[:, k, cq * 128:cq * 128 + 128],
                                xct2[par][:, k, :], start=(k == 0), stop=(k == 7)))
                            for k in range(8)
                        ]
                        ctx.mm_group(fns, reads=[wkgsb_b, xct2_b[par]], writes=[psP_b[pp]])
                        ctx.emit(ctx.act, lambda e, cq=cq, tr=tr, pp=pp: e.activation(
                            gkraw[:, cq, tr * 512:tr * 512 + 512], psP[pp][:, :], AF.Copy),
                            reads=[psP_b[pp]], writes=[gkraw_b[cq]])
                    for cv in range(4):
                        pp = cv % 2
                        fns = [
                            (lambda e, k=k, cv=cv, pp=pp, par=par: e.matmul(
                                psP[pp][:, :], wvsb[:, k, cv * 128:cv * 128 + 128],
                                xct2[par][:, k, :], start=(k == 0), stop=(k == 7)))
                            for k in range(8)
                        ]
                        ctx.mm_group(fns, reads=[wvsb_b, xct2_b[par]], writes=[psP_b[pp]])
                        ctx.emit(ctx.act, lambda e, cv=cv, tr=tr, pp=pp: e.activation(
                            vsb[:, cv, tr * 512:tr * 512 + 512], psP[pp][:, :], AF.Copy),
                            reads=[psP_b[pp]], writes=[vsb_b])
                ga = ctx.sb(sa, f"g_ga{hg}", [128, TOK], F32)
                ga_b = Buf("g_ga")
                gb = ctx.sb(sa, f"g_gb{hg}", [128, TOK], F32)
                gb_b = Buf("g_gb")
                for cq in range(2):
                    G = gkraw[:, cq, :]
                    Gb = gkraw_b[cq]
                    ctx.emit(ctx.act, lambda e, G=G: e.activation(
                        ga[:, :], G, AF.Abs),
                        reads=[Gb], writes=[ga_b])
                    ctx.emit(ctx.act, lambda e: e.activation(gb[:, :], ga[:, :], AF.Exp,
                                                             scale=-1.0),
                             reads=[ga_b], writes=[gb_b])
                    ctx.emit(ctx.dve, lambda e: e.tensor_scalar(
                        gb[:, :], gb[:, :], 1.0, None, ALU.add),
                        writes=[gb_b])
                    ctx.emit(ctx.act, lambda e: e.activation(ga[:, :], gb[:, :], AF.Ln),
                             reads=[gb_b], writes=[ga_b])
                    ctx.emit(ctx.dve, lambda e, G=G: e.tensor_scalar(
                        gb[:, :], G, 0.0, None, ALU.min),
                        reads=[Gb], writes=[gb_b])
                    ctx.emit(ctx.dve, lambda e, G=G: e.tensor_tensor(
                        G, gb[:, :], ga[:, :], ALU.subtract),
                        reads=[gb_b, ga_b], writes=[Gb])
                    ctx.emit(ctx.act, lambda e, G=G: e.activation(
                        ga[:, :], G, AF.Exp, scale=1.0 / 16.0),
                        reads=[Gb], writes=[ga_b])
                    ctx.emit(ctx.dve, lambda e, cq=cq: e.tensor_scalar(
                        ksb[:, cq, :], ga[:, :], -1.0, 1.0, ALU.mult, ALU.add),
                        reads=[ga_b], writes=[ksb_b])
                    for b in range(1, 16):
                        ctx.emit(ctx.dve, lambda e, G=G, b=b: e.tensor_tensor(
                            G[:, b * 256:(b + 1) * 256], G[:, b * 256:(b + 1) * 256],
                            G[:, (b - 1) * 256:b * 256], ALU.add),
                            writes=[Gb])
                augsb = ctx.sb(sa, f"g_augsb{hg}", [128, 2], F32)
                augsb_b = Buf("g_augsb")
                augrow = ctx.sb(sa, f"g_augrow{hg}", [1, 512], F32)
                augrow_b = Buf("g_augrow")
                ctx.emit(ctx.sp, lambda e: e.dma_start(out=augrow[:, :], in_=augb[:, :]),
                         writes=[augrow_b], inc=16)
                for cq in range(2):
                    ctx.mm_group([lambda e, cq=cq, hg=hg: e.matmul(
                        psP[0][:, 0:1], augrow[0:1, 256 * hg + 128 * cq:256 * hg + 128 * cq + 128],
                        ones_sb[0:1, 0:1], start=True, stop=True)],
                        reads=[augrow_b, ones_b], writes=[psP_b[0]])
                    ctx.emit(ctx.dve, lambda e, cq=cq: e.tensor_scalar(
                        augsb[:, cq:cq + 1], psP[0][:, 0:1], 1.0 / RSQ32, None, ALU.mult),
                        reads=[psP_b[0]], writes=[augsb_b])
                prodb = ctx.sb(sa, f"g_prodb{hg}", [128, 512], BF16)
                prodb_b = Buf("g_prodb")
                proda = ctx.sb(sa, f"g_proda{hg}", [128, 512], BF16)
                proda_b = Buf("g_proda")
                for tr in range(8):
                    for cq in range(2):
                        ctx.emit(ctx.dve, lambda e, cq=cq, tr=tr: e.tensor_tensor(
                            prodb[:, :], qsb[:, cq, tr * 512:tr * 512 + 512],
                            ksb[:, cq, tr * 512:tr * 512 + 512], ALU.mult),
                            reads=[qsb_b, ksb_b], writes=[prodb_b])
                        ctx.emit(ctx.act, lambda e, cq=cq: e.activation(
                            proda[:, :], prodb[:, :], AF.Identity,
                            scale=augsb[:, cq:cq + 1]),
                            reads=[prodb_b], sreads=[augsb_b], writes=[proda_b])
                        ctx.mm_group([lambda e, cq=cq: e.matmul(
                            psA8[:, :], eqk_sb[:, cq, :], proda[:, :],
                            start=(cq == 0), stop=(cq == 1), skip_group_check=True)],
                            reads=[eqk_b, proda_b], writes=[psA8_b])
                    ctx.emit(ctx.act, lambda e, tr=tr: e.activation(
                        augw[:, tr * 512:tr * 512 + 512], psA8[:, :], AF.Copy),
                        reads=[psA8_b], writes=[augw_b])
                for cq in range(2):
                    G = gkraw[:, cq, :]
                    Gb = gkraw_b[cq]
                    ctx.emit(ctx.act, lambda e, G=G: e.activation(
                        ga[:, :], G, AF.Exp, scale=1.0 / 16.0),
                        reads=[Gb], writes=[ga_b])
                    ctx.emit(ctx.dve, lambda e, cq=cq: e.tensor_tensor(
                        qsb[:, cq, :], qsb[:, cq, :], ga[:, :], ALU.mult),
                        reads=[ga_b], writes=[qsb_b])
                    ctx.emit(ctx.act, lambda e, G=G: e.activation(
                        gb[:, :], G, AF.Exp, scale=-1.0 / 16.0),
                        reads=[Gb], writes=[gb_b])
                    ctx.emit(ctx.dve, lambda e, cq=cq: e.tensor_tensor(
                        ksb[:, cq, :], ksb[:, cq, :], gb[:, :], ALU.mult),
                        reads=[gb_b], writes=[ksb_b])

            og = ctx.sb(sg, f"g_og{hg}", [128, 4, TOK], F32)
            og_b = [Buf(f"g_og{hg}_{cv}") for cv in range(4)]
            for cv in range(4):
                ctx.emit(ctx.dve, lambda e, cv=cv: e.memset(og[:, cv, :], 0.0),
                         writes=[og_b[cv]])
            prodbf = [ctx.sb(sg, f"g_prodbf{hg}{i}", [128, 1024], BF16) for i in range(2)]
            prodbf_b = [Buf(f"g_prodbf{hg}{i}") for i in range(2)]
            asb = [ctx.sb(sg, f"g_asb{hg}{i}", [8, 1024], BF16) for i in range(2)]
            asb_b = [Buf(f"g_asb{hg}{i}") for i in range(2)]
            tmpf = [ctx.sb(sg, f"g_tmpf{hg}{i}", [128, 1024], F32) for i in range(2)]
            tmpf_b = [Buf(f"g_tmpf{hg}{i}") for i in range(2)]
            fold = [ctx.sb(sg, f"g_fold{hg}{i}", [128, 256], F32) for i in range(2)]
            fold_b = [Buf(f"g_fold{hg}{i}") for i in range(2)]
            psA8w = ctx.psum(sg, f"g_psA8w{hg}", [8, 1024], F32)
            psA8w_b = Buf(f"g_psA8w{hg}")
            psAbw = [ctx.psum(sg, f"g_psAbw{hg}{i}", [128, 1024], F32) for i in range(2)]
            psAbw_b = [Buf(f"g_psAbw{hg}{i}") for i in range(2)]
            pidx = 0
            for t in range(16):
                nb_all = t + 1
                for tb in range((nb_all + 3) // 4):
                    tau0 = tb * 4
                    nb = min(4, nb_all - tau0)
                    w = nb * 256
                    pp = pidx % 2
                    pidx += 1
                    for cq in range(2):
                        qv = qsb[:, cq, t * 256:t * 256 + 256]
                        qbc = bass.AP(qv.tensor, qv.offset,
                                      [list(qv.ap[0])] + [[0, nb]] + [list(qv.ap[-1])])
                        ctx.emit(ctx.dve, lambda e, cq=cq, pp=pp, qbc=qbc, tau0=tau0, nb=nb, w=w:
                                 e.tensor_tensor(
                                     prodbf[pp][:, 0:w].rearrange("p (a b) -> p a b", b=256),
                                     qbc,
                                     ksb[:, cq, tau0 * 256:tau0 * 256 + w].rearrange(
                                         "p (a b) -> p a b", b=256),
                                     ALU.mult),
                                 reads=[qsb_b, ksb_b], writes=[prodbf_b[pp]])
                        fns = []
                        for o5 in range(0, w, 512):
                            w5 = min(512, w - o5)
                            fns.append(lambda e, cq=cq, pp=pp, o5=o5, w5=w5: e.matmul(
                                psA8w[0:8, o5:o5 + w5], eqk_sb[:, cq, :],
                                prodbf[pp][:, o5:o5 + w5],
                                start=(cq == 0), stop=(cq == 1), skip_group_check=True))
                        ctx.mm_group(fns, reads=[eqk_b, prodbf_b[pp]], writes=[psA8w_b])
                    ctx.emit(ctx.act, lambda e, pp=pp, w=w: e.activation(
                        asb[pp][:, 0:w], psA8w[0:8, 0:w], AF.Copy),
                        reads=[psA8w_b], writes=[asb_b[pp]])
                    for cv in range(4):
                        pb = cv % 2
                        fns = []
                        for o5 in range(0, w, 512):
                            w5 = min(512, w - o5)
                            fns.append(lambda e, cv=cv, pp=pp, pb=pb, o5=o5, w5=w5: e.matmul(
                                psAbw[pb][:, o5:o5 + w5], ebcA_sb[:, cv, :],
                                asb[pp][:, o5:o5 + w5], start=True, stop=True))
                        ctx.mm_group(fns, reads=[ebcA_b, asb_b[pp]], writes=[psAbw_b[pb]])
                        ctx.emit(ctx.dve, lambda e, cv=cv, tau0=tau0, pb=pb, w=w:
                                 e.tensor_tensor(
                                     tmpf[pb][:, 0:w], psAbw[pb][:, 0:w],
                                     vsb[:, cv, tau0 * 256:tau0 * 256 + w], ALU.mult),
                                 reads=[psAbw_b[pb], vsb_b], writes=[tmpf_b[pb]])
                        if nb == 1:
                            ctx.emit(ctx.pool, lambda e, cv=cv, t=t, pb=pb: e.tensor_tensor(
                                og[:, cv, t * 256:t * 256 + 256],
                                og[:, cv, t * 256:t * 256 + 256], tmpf[pb][:, 0:256],
                                ALU.add),
                                reads=[tmpf_b[pb]], writes=[og_b[cv]])
                        else:
                            ctx.emit(ctx.dve, lambda e, pb=pb: e.tensor_tensor(
                                fold[pb][:, :], tmpf[pb][:, 0:256], tmpf[pb][:, 256:512],
                                ALU.add),
                                reads=[tmpf_b[pb]], writes=[fold_b[pb]])
                            for ex2 in range(2, nb):
                                ctx.emit(ctx.dve, lambda e, pb=pb, ex2=ex2: e.tensor_tensor(
                                    fold[pb][:, :], fold[pb][:, :],
                                    tmpf[pb][:, ex2 * 256:ex2 * 256 + 256], ALU.add),
                                    reads=[tmpf_b[pb]], writes=[fold_b[pb]])
                            ctx.emit(ctx.pool, lambda e, cv=cv, t=t, pb=pb: e.tensor_tensor(
                                og[:, cv, t * 256:t * 256 + 256],
                                og[:, cv, t * 256:t * 256 + 256], fold[pb][:, :], ALU.add),
                                reads=[fold_b[pb]], writes=[og_b[cv]])
            sgf = ctx.sb(sg, f"g_sgf{hg}", [128, 512], F32)
            sgf_b = Buf(f"g_sgf{hg}")
            sgs = ctx.sb(sg, f"g_sgs{hg}", [128, 512], F32)
            sgs_b = Buf(f"g_sgs{hg}")
            for tr in range(8):
                for cv in range(4):
                    pb = cv % 2
                    ctx.mm_group([lambda e, cv=cv, tr=tr, pb=pb: e.matmul(
                        psAbw[pb][:, 0:512], ebcA_sb[:, cv, :], augw[:, tr * 512:tr * 512 + 512],
                        start=True, stop=True)],
                        reads=[ebcA_b, augw_b], writes=[psAbw_b[pb]])
                    ctx.emit(ctx.dve, lambda e, cv=cv, tr=tr, pb=pb: e.tensor_tensor(
                        sgf[:, :], psAbw[pb][:, 0:512], vsb[:, cv, tr * 512:tr * 512 + 512],
                        ALU.mult),
                        reads=[psAbw_b[pb], vsb_b], writes=[sgf_b])
                    ctx.emit(ctx.act, lambda e: e.activation(sgs[:, :], sgf[:, :], AF.Sigmoid),
                             reads=[sgf_b], writes=[sgs_b])
                    ctx.emit(ctx.pool, lambda e, cv=cv, tr=tr: e.tensor_tensor(
                        og[:, cv, tr * 512:tr * 512 + 512],
                        og[:, cv, tr * 512:tr * 512 + 512], sgs[:, :], ALU.add),
                        reads=[sgs_b], writes=[og_b[cv]])
            psMu = psA8w[0:8, 0:512]
            psMu_b = psA8w_b
            psMs = psA8w[0:8, 512:1024]
            psMs_b = psA8w_b
            psR = psAbw[0][:, 0:512]
            psR_b = psAbw_b[0]
            psM2 = psAbw[1][:, 0:512]
            psM2_b = psAbw_b[1]
            sqf = tmpf[0][:, 0:512]
            sqf_b = tmpf_b[0]
            msb = tmpf[1][0:8, 0:512]
            msb_b = tmpf_b[1]
            msq = tmpf[1][0:8, 512:1024]
            msq_b = tmpf_b[1]
            var = ctx.sb(sg, f"g_var{hg}", [8, 512], F32)
            var_b = Buf(f"g_var{hg}")
            sdv = ctx.sb(sg, f"g_sdv{hg}", [8, 512], F32)
            sdv_b = Buf(f"g_sdv{hg}")
            rst = ctx.sb(sg, f"g_rst{hg}", [8, 512], F32)
            rst_b = Buf(f"g_rst{hg}")
            mrs = ctx.sb(sg, f"g_mrs{hg}", [8, 512], F32)
            mrs_b = Buf(f"g_mrs{hg}")
            onb = prodbf[0][:, 0:512]
            onb_b = prodbf_b[0]
            o1f = ctx.sb(sg, f"g_o1f{hg}", [128, 512], F32)
            o1f_b = Buf(f"g_o1f{hg}")
            for tr in range(8):
                fns = [
                    (lambda e, cv=cv, tr=tr: e.matmul(
                        psMu, esumV_sb[:, cv, :], og[:, cv, tr * 512:tr * 512 + 512],
                        start=(cv == 0), stop=(cv == 3)))
                    for cv in range(4)
                ]
                ctx.mm_group(fns, reads=[esumV_b] + og_b, writes=[psMu_b])
                for cv in range(4):
                    ctx.emit(ctx.dve, lambda e, cv=cv, tr=tr: e.tensor_tensor(
                        sqf, og[:, cv, tr * 512:tr * 512 + 512],
                        og[:, cv, tr * 512:tr * 512 + 512], ALU.mult),
                        reads=[og_b[cv]], writes=[sqf_b])
                    ctx.mm_group([lambda e, cv=cv: e.matmul(
                        psMs, esumV_sb[:, cv, :], sqf,
                        start=(cv == 0), stop=(cv == 3), skip_group_check=True)],
                        reads=[esumV_b, sqf_b], writes=[psMs_b])
                ctx.emit(ctx.act, lambda e: e.activation(msb, psMu, AF.Copy,
                                                         scale=1.0 / 64.0),
                         reads=[psMu_b], writes=[msb_b])
                ctx.emit(ctx.act, lambda e: e.activation(msq, psMu, AF.Square,
                                                         scale=1.0 / 64.0),
                         reads=[psMu_b], writes=[msq_b])
                ctx.emit(ctx.dve, lambda e: e.tensor_scalar(
                    var[:, :], psMs, 1.0 / 64.0, 1e-5, ALU.mult, ALU.add),
                    reads=[psMs_b], writes=[var_b])
                ctx.emit(ctx.dve, lambda e: e.tensor_tensor(
                    var[:, :], var[:, :], msq, ALU.subtract),
                    reads=[msq_b], writes=[var_b])
                ctx.emit(ctx.act, lambda e: e.activation(sdv[:, :], var[:, :], AF.Sqrt),
                         reads=[var_b], writes=[sdv_b])
                ctx.emit(ctx.dve, lambda e: e.reciprocal(rst[:, :], sdv[:, :]),
                         reads=[sdv_b], writes=[rst_b])
                ctx.emit(ctx.dve, lambda e: e.tensor_tensor(
                    mrs[:, :], msb, rst[:, :], ALU.mult),
                    reads=[msb_b, rst_b], writes=[mrs_b])
                for cv in range(4):
                    ctx.mm_group([lambda e, cv=cv: e.matmul(
                        psR, ebcVf_sb[:, cv, :], rst[:, :], start=True, stop=True)],
                        reads=[ebcVf_b, rst_b], writes=[psR_b])
                    ctx.mm_group([lambda e, cv=cv: e.matmul(
                        psM2, ebcVf_sb[:, cv, :], mrs[:, :], start=True, stop=True)],
                        reads=[ebcVf_b, mrs_b], writes=[psM2_b])
                    ctx.emit(ctx.dve, lambda e, cv=cv, tr=tr: e.tensor_tensor(
                        o1f[:, :], og[:, cv, tr * 512:tr * 512 + 512], psR, ALU.mult),
                        reads=[og_b[cv], psR_b], writes=[o1f_b])
                    ctx.emit(ctx.dve, lambda e: e.tensor_tensor(
                        onb, o1f[:, :], psM2, ALU.subtract),
                        reads=[o1f_b, psM2_b], writes=[onb_b])
                    ctx.emit(ctx.pool, lambda e, cv=cv, tr=tr, hg=hg: e.dma_start(
                        out=on_d[hg * 512 + cv * 128:hg * 512 + cv * 128 + 128,
                                 tr * 512:tr * 512 + 512],
                        in_=onb),
                        reads=[onb_b], inc=16, sem=s_st)

    with ctx.scope() as s4:
        wosb = ctx.sb(s4, "g_wosb", [128, 8, D], BF16)
        wosb_b = Buf("g_wosb")
        ctx.emit(ctx.pool, lambda e: e.dma_start(
            out=wosb[:, :, :], in_=wout.ap().rearrange("(c p) n -> p c n", p=128)),
            writes=[wosb_b], inc=16)
        gst = [ctx.sb(s4, f"g_gst{i}", [128, 8, 512], BF16) for i in range(2)]
        gst_b = [Buf(f"g_gst{i}") for i in range(2)]
        ont = [ctx.sb(s4, f"g_ont{i}", [128, 8, 512], BF16) for i in range(2)]
        ont_b = [Buf(f"g_ont{i}") for i in range(2)]
        gated = ctx.sb(s4, "g_gated", [128, 8, 512], BF16)
        gated_b = Buf("g_gated")
        xt4 = [ctx.sb(s4, f"g_xt4{i}", [128, D], F32) for i in range(2)]
        xt4_b = [Buf(f"g_xt4{i}") for i in range(2)]
        xn4 = [ctx.sb(s4, f"g_xn4{i}", [128, D], F32) for i in range(2)]
        xn4_b = [Buf(f"g_xn4{i}") for i in range(2)]
        psF = [ctx.psum(s4, f"g_psF{i}", [128, 512], F32) for i in range(2)]
        psF_b = [Buf(f"g_psF{i}") for i in range(2)]
        for tr in range(8):
            par = tr % 2
            for c8 in range(8):
                ctx.emit(ctx.sp, lambda e, c8=c8, tr=tr, par=par: e.dma_start(
                    out=gst[par][:, c8, :], in_=gsil_d[c8 * 128:c8 * 128 + 128,
                                                       tr * 512:tr * 512 + 512]),
                    writes=[gst_b[par]], inc=16, sem=s_sc)
                ctx.emit(ctx.sp, lambda e, c8=c8, tr=tr, par=par: e.dma_start(
                    out=ont[par][:, c8, :], in_=on_d[c8 * 128:c8 * 128 + 128,
                                                     tr * 512:tr * 512 + 512]),
                    writes=[ont_b[par]], inc=16, sem=s_sc)
            ctx.emit(ctx.dve, lambda e, par=par: e.tensor_tensor(
                gated[:, :, :], gst[par][:, :, :], ont[par][:, :, :], ALU.mult),
                reads=[gst_b[par], ont_b[par]], writes=[gated_b])
            for s4i in range(4):
                sub = tr * 4 + s4i
                r0 = sub * 128
                spar = s4i % 2
                ctx.emit(ctx.sp, lambda e, r0=r0, spar=spar: e.dma_start(
                    out=xt4[spar][:, :], in_=xsrc[r0:r0 + 128, :]),
                    writes=[xt4_b[spar]], inc=16, sem=s_xl)
                for half in range(2):
                    pp = half
                    fns = [
                        (lambda e, c8=c8, s4i=s4i, half=half, pp=pp: e.matmul(
                            psF[pp][:, :], gated[:, c8, s4i * 128:s4i * 128 + 128],
                            wosb[:, c8, half * 512:half * 512 + 512],
                            start=(c8 == 0), stop=(c8 == 7)))
                        for c8 in range(8)
                    ]
                    ctx.mm_group(fns, reads=[gated_b, wosb_b], writes=[psF_b[pp]])
                    ctx.emit(ctx.dve, lambda e, half=half, pp=pp, spar=spar: e.tensor_tensor(
                        xn4[spar][:, half * 512:half * 512 + 512], psF[pp][:, :],
                        xt4[spar][:, half * 512:half * 512 + 512], ALU.add),
                        reads=[psF_b[pp], xt4_b[spar]], writes=[xn4_b[spar]])
                ctx.emit(ctx.pool, lambda e, r0=r0, spar=spar: e.dma_start(
                    out=xdst[r0:r0 + 128, :], in_=xn4[spar][:, :]),
                    reads=[xn4_b[spar]], inc=16, sem=s_st)


def _runA(inputs, xa):
    nc = build_launch_A()
    w1 = np.ascontiguousarray(inputs["L0_ffn_W1"].T)
    w2 = np.ascontiguousarray(inputs["L0_ffn_W2"].T)
    common = {
        "w1t": w1, "w2t": w2,
        "b1": inputs["L0_ffn_b1"].reshape(1, -1),
        "b2": inputs["L0_ffn_b2"].reshape(1, -1),
        "wk": inputs["L0_attn_Wk"], "wq": inputs["L0_attn_Wq"],
        "wo": inputs["L0_attn_Wo"], "bo": inputs["L0_attn_bo"].reshape(1, -1),
        "ident": np.eye(128, dtype=ml_dtypes.bfloat16),
        "onesf": np.ones((128, 128), np.float32),
    }
    in_maps = []
    for r in range(R):
        m = dict(common)
        m["xin"] = np.ascontiguousarray(xa[2 * r:2 * r + 2].reshape(TOK, D))
        in_maps.append(m)
    res = run_bass_kernel_spmd(nc, in_maps, core_ids=list(range(R)))
    out = np.empty((16, 2048, D), np.float32)
    for r in range(R):
        out[2 * r:2 * r + 2] = res.results[r]["xout"].reshape(2, 2048, D)
    return out


def _gla_consts():
    eqk = np.zeros((2 * 128, 8), dtype=ml_dtypes.bfloat16)
    for c in range(2):
        for r in range(128):
            eqk[c * 128 + r, 4 * c + r // 32] = 1
    ebcA = np.zeros((4 * 8, 128), dtype=ml_dtypes.bfloat16)
    esumV = np.zeros((4 * 128, 8), dtype=np.float32)
    ebcVf = np.zeros((4 * 8, 128), dtype=np.float32)
    for c in range(4):
        for p in range(128):
            h = 2 * c + p // 64
            ebcA[c * 8 + h, p] = 1
            ebcVf[c * 8 + h, p] = 1
            esumV[c * 128 + p, h] = 1
    return eqk, ebcA, esumV, ebcVf


def _runB(inputs, xb):
    nc = build_launch_B()
    eqk, ebcA, esumV, ebcVf = _gla_consts()
    common = {
        "wgm": np.ascontiguousarray(inputs["L1_ffn_Wg"].T),
        "w1m": np.ascontiguousarray(
            inputs["L1_ffn_W1"].reshape(8, 8, 256, D).transpose(0, 1, 3, 2)
        ).reshape(8 * 8 * D, 256),
        "w2m": np.ascontiguousarray(
            inputs["L1_ffn_W2"].reshape(8, 8, 256, D).transpose(0, 1, 3, 2)
        ).reshape(8 * 8 * D, 256),
        "w3m": np.ascontiguousarray(inputs["L1_ffn_W3"].transpose(0, 2, 1)).reshape(8 * 2048, D),
        "convw": inputs["L1_attn_conv_w"],
        "wq1": inputs["L1_attn_Wq"], "wkg": inputs["L1_attn_Wkg"],
        "wv1": inputs["L1_attn_Wv"], "wg1": inputs["L1_attn_Wg"],
        "bg1": inputs["L1_attn_bg"].reshape(1, -1),
        "wout": inputs["L1_attn_Wout"],
        "augb": inputs["L1_attn_aug_balance"].reshape(1, -1),
        "ident": np.eye(128, dtype=ml_dtypes.bfloat16),
        "onesf": np.ones((128, 128), np.float32),
        "onesb": np.ones((128, 128), ml_dtypes.bfloat16),
        "identf": np.eye(128, dtype=np.float32),
        "eqk": eqk, "ebcA": ebcA, "esumV": esumV, "ebcVf": ebcVf,
    }
    in_maps = []
    for r in range(R):
        m = dict(common)
        m["xin"] = np.ascontiguousarray(
            xb[:, 256 * r:256 * (r + 1), :]).reshape(TOK, D)
        in_maps.append(m)
    res = run_bass_kernel_spmd(nc, in_maps, core_ids=list(range(R)))
    out = np.empty((16, 2048, D), np.float32)
    for r in range(R):
        out[:, 256 * r:256 * (r + 1), :] = res.results[r]["xout"].reshape(16, 256, D)
    return out


def kernel(**inputs):
    inputs = {k: np.asarray(v) for k, v in inputs.items()}
    xa = inputs["L0_attn_x"]
    x_afterA = _runA(inputs, xa)
    return _runB(inputs, x_afterA)


# version 06 (final)
# speedup vs baseline: 1.0482x; 1.0482x over previous
import contextlib
import numpy as np
import ml_dtypes
import concourse.bass as bass
import concourse.mybir as mybir
from concourse.bass_utils import run_bass_kernel_spmd

F32 = mybir.dt.float32
BF16 = mybir.dt.bfloat16
AF = mybir.ActivationFunctionType
ALU = mybir.AluOpType
AX = mybir.AxisListType

R = 8
D = 1024
TOK = 4096
NSUB = TOK // 128

PHASES_A = ["p1", "p2", "p3"]
PHASES_B = ["p4", "p5", "p6"]
MOE_EXPERTS = 8


class _Sem:
    def __init__(self, nc, name, unordered=False):
        self.h = nc.alloc_semaphore(name)
        self.count = 0
        self.unordered = unordered


class Buf:
    __slots__ = ("prod", "prod_eng", "readers", "name")

    def __init__(self, name=""):
        self.prod = None
        self.prod_eng = None
        self.readers = []
        self.name = name


class _Eng:
    def __init__(self, nc, engine, name, unordered=False):
        self.e = engine
        self.name = name
        self.sem = _Sem(nc, f"s_{name}", unordered=unordered)
        self.waited = {}

    def wait_ms(self, ms):
        sem, val = ms
        if sem is self.sem and self.name == "pe":
            return
        if sem.unordered:
            val = sem.count
        if self.waited.get(id(sem), 0) >= val:
            return
        assert val <= sem.count, "wait on future sem value"
        self.e.wait_ge(sem.h, val)
        self.waited[id(sem)] = val


class Ctx:
    def __init__(self, nc):
        self.nc = nc
        self.pe = _Eng(nc, nc.tensor, "pe")
        self.dve = _Eng(nc, nc.vector, "dve")
        self.act = _Eng(nc, nc.scalar, "act")
        self.pool = _Eng(nc, nc.gpsimd, "pool", unordered=True)
        self.sp = _Eng(nc, nc.sync, "sp", unordered=True)
        self.engines = [self.pe, self.dve, self.act, self.pool, self.sp]
        self.all_sems = [e.sem for e in self.engines]

    def stream_sem(self, name):
        s = _Sem(self.nc, name, unordered=True)
        self.all_sems.append(s)
        return s

    def emit(self, eng, fn, reads=(), writes=(), sreads=(), inc=1, sem=None):
        for b in sreads:
            if b.prod is not None:
                assert b.prod_eng is not eng, (
                    f"scalar-ptr operand {b.name} produced on same engine {eng.name}")
        for b in list(reads) + list(sreads):
            if b.prod is not None:
                eng.wait_ms(b.prod)
        for b in writes:
            if b.prod is not None:
                eng.wait_ms(b.prod)
            for ms in b.readers:
                eng.wait_ms(ms)
        instr = fn(eng.e)
        s = sem if sem is not None else eng.sem
        instr.then_inc(s.h, inc)
        s.count += inc
        ms = (s, s.count)
        for b in writes:
            b.prod = ms
            b.prod_eng = eng
            b.readers = []
        for b in list(reads) + list(sreads):
            b.readers.append(ms)
        return instr

    def mm_group(self, fns, reads=(), writes=()):
        eng = self.pe
        for b in reads:
            if b.prod is not None:
                eng.wait_ms(b.prod)
        for b in writes:
            if b.prod is not None:
                eng.wait_ms(b.prod)
            for ms in b.readers:
                eng.wait_ms(ms)
        instr = None
        for fn in fns:
            instr = fn(self.nc.tensor)
        instr.then_inc(eng.sem.h, 1)
        eng.sem.count += 1
        ms = (eng.sem, eng.sem.count)
        for b in writes:
            b.prod = ms
            b.prod_eng = eng
            b.readers = []
        for b in reads:
            b.readers.append(ms)
        return instr

    @contextlib.contextmanager
    def scope(self):
        parent = getattr(self, "_stack", None)
        with contextlib.ExitStack() as stack:
            self._stack = stack
            yield stack
            self.barrier_all()
        self._stack = parent

    def sb(self, stack, name, shape, dtype):
        return stack.enter_context(self.nc.sbuf_tensor(name, shape, dtype))

    def psum(self, stack, name, shape, dtype):
        return stack.enter_context(self.nc.psum_tensor(name, shape, dtype))

    def barrier_all(self):
        for e1 in self.engines:
            for s in self.all_sems:
                if s is not e1.sem and s.count > 0:
                    e1.wait_ms((s, s.count))


def _layernorm_subtile(ctx, xt, xt_b, stats, xln_dst_col, xlnfm, xlnfm_b, ident, ident_b,
                      ptr, ptr_b, out_bf16=True):
    nc = ctx.nc
    s = stats
    ctx.emit(ctx.dve, lambda e: e.reduce_sum(s["sum"][0][:, :], xt[:, :], axis=AX.X),
             reads=[xt_b], writes=[s["sum"][1]])
    ctx.emit(ctx.dve, lambda e: e.tensor_tensor(s["xsq"][0][:, :], xt[:, :], xt[:, :], ALU.mult),
             reads=[xt_b], writes=[s["xsq"][1]])
    ctx.emit(ctx.dve, lambda e: e.reduce_sum(s["ssq"][0][:, :], s["xsq"][0][:, :], axis=AX.X),
             reads=[s["xsq"][1]], writes=[s["ssq"][1]])
    ctx.emit(ctx.act, lambda e: e.activation(s["nmean"][0][:, :], s["sum"][0][:, :], AF.Copy,
                                             scale=-1.0 / D),
             reads=[s["sum"][1]], writes=[s["nmean"][1]])
    ctx.emit(ctx.act, lambda e: e.activation(s["msq"][0][:, :], s["sum"][0][:, :], AF.Square,
                                             scale=1.0 / D),
             reads=[s["sum"][1]], writes=[s["msq"][1]])
    ctx.emit(ctx.dve, lambda e: e.tensor_scalar(s["sqv"][0][:, :], s["ssq"][0][:, :],
                                                1.0 / D, 1e-5, ALU.mult, ALU.add),
             reads=[s["ssq"][1]], writes=[s["sqv"][1]])
    ctx.emit(ctx.dve, lambda e: e.tensor_tensor(s["var"][0][:, :], s["sqv"][0][:, :],
                                                s["msq"][0][:, :], ALU.subtract),
             reads=[s["sqv"][1], s["msq"][1]], writes=[s["var"][1]])
    ctx.emit(ctx.act, lambda e: e.activation(s["sq"][0][:, :], s["var"][0][:, :], AF.Sqrt),
             reads=[s["var"][1]], writes=[s["sq"][1]])
    ctx.emit(ctx.dve, lambda e: e.reciprocal(s["rstd"][0][:, :], s["sq"][0][:, :]),
             reads=[s["sq"][1]], writes=[s["rstd"][1]])
    ctx.emit(ctx.dve, lambda e: e.tensor_tensor(s["mb"][0][:, :], s["nmean"][0][:, :],
                                                s["rstd"][0][:, :], ALU.mult),
             reads=[s["nmean"][1], s["rstd"][1]], writes=[s["mb"][1]])
    ctx.emit(ctx.act, lambda e: e.activation(s["xln_tm"][0][:, :], xt[:, :], AF.Identity,
                                             scale=s["rstd"][0][:, :], bias=s["mb"][0][:, :]),
             reads=[xt_b], sreads=[s["rstd"][1], s["mb"][1]], writes=[s["xln_tm"][1]])
    fns = [
        (lambda e, kk=k: e.transpose(ptr[:, kk, :],
                                     s["xln_tm"][0][:, kk * 128:kk * 128 + 128],
                                     ident[:, :]))
        for k in range(8)
    ]
    ctx.mm_group(fns, reads=[s["xln_tm"][1], ident_b], writes=[ptr_b])
    ctx.emit(ctx.dve, lambda e: e.tensor_copy(
        xlnfm[:, :, xln_dst_col:xln_dst_col + 128], ptr[:, :, :]),
        reads=[ptr_b], writes=[xlnfm_b])


def _alloc_ln_stats(ctx, tag, shared_xsq=None):
    names = ["sum", "xsq", "ssq", "nmean", "msq", "sqv", "var", "sq", "rstd", "mb", "xln_tm"]
    out = {}
    for n in names:
        if n == "xsq":
            if shared_xsq is not None:
                out[n] = shared_xsq
                continue
            t = ctx.sb(ctx._stack, f"{tag}_{n}", [128, D], F32)
        elif n == "xln_tm":
            t = ctx.sb(ctx._stack, f"{tag}_{n}", [128, D], BF16)
        else:
            t = ctx.sb(ctx._stack, f"{tag}_{n}", [128, 1], F32)
        out[n] = (t, Buf(f"{tag}_{n}"))
    return out


def _alloc_ln_slots(ctx, tag):
    s0 = _alloc_ln_stats(ctx, f"{tag}0")
    s1 = _alloc_ln_stats(ctx, f"{tag}1", shared_xsq=s0["xsq"])
    return [s0, s1]


def _bcast_row_to_partitions(ctx, src_row_ap, src_b, ones, ones_b, ps, ps_b, dst, dst_b,
                             ncols, scale=None):
    nc = ctx.nc
    nmm = (ncols + 511) // 512
    for i in range(nmm):
        c0, c1 = i * 512, min(ncols, (i + 1) * 512)
        w = c1 - c0
        ctx.mm_group([lambda e, c0=c0, c1=c1, w=w: e.matmul(ps[:, 0:w], ones[0:1, 0:128],
                                                            src_row_ap[0:1, c0:c1],
                                                            start=True, stop=True)],
                     reads=[src_b, ones_b], writes=[ps_b])
        if scale is None:
            ctx.emit(ctx.dve, lambda e, c0=c0, c1=c1, w=w: e.tensor_copy(dst[:, c0:c1], ps[:, 0:w]),
                     reads=[ps_b], writes=[dst_b])
        else:
            ctx.emit(ctx.dve, lambda e, c0=c0, c1=c1, w=w: e.tensor_scalar(
                dst[:, c0:c1], ps[:, 0:w], scale, None, ALU.mult),
                reads=[ps_b], writes=[dst_b])


def build_launch_A():
    nc = bass.Bass("TRN2", target_bir_lowering=False, debug=False)

    xin = nc.dram_tensor("xin", [TOK, D], F32, kind="ExternalInput")
    w1t = nc.dram_tensor("w1t", [D, 4096], F32, kind="ExternalInput")
    w2t = nc.dram_tensor("w2t", [4096, D], F32, kind="ExternalInput")
    b1 = nc.dram_tensor("b1", [1, 4096], F32, kind="ExternalInput")
    b2 = nc.dram_tensor("b2", [1, D], F32, kind="ExternalInput")
    wk = nc.dram_tensor("wk", [D, D], F32, kind="ExternalInput")
    wq = nc.dram_tensor("wq", [D, D], F32, kind="ExternalInput")
    wo = nc.dram_tensor("wo", [D, D], F32, kind="ExternalInput")
    bo = nc.dram_tensor("bo", [1, D], F32, kind="ExternalInput")
    ident = nc.dram_tensor("ident", [128, 128], BF16, kind="ExternalInput")
    onesf = nc.dram_tensor("onesf", [128, 128], F32, kind="ExternalInput")

    xout = nc.dram_tensor("xout", [TOK, D], F32, kind="ExternalOutput")

    w1t_bf = nc.dram_tensor("w1t_bf", [D, 4096], BF16)
    w2t_bf = nc.dram_tensor("w2t_bf", [4096, D], BF16)
    xs1 = nc.dram_tensor("xs1", [TOK, D], F32)
    xs2 = nc.dram_tensor("xs2", [TOK, D], F32)
    xln_d = nc.dram_tensor("xln_d", [D, TOK], BF16)

    ctx = Ctx(nc)

    with nc.Block():
        id_sb = nc.alloc_sbuf_tensor("id_sb", [128, 128], BF16)
        ones_sb = nc.alloc_sbuf_tensor("ones_sb", [128, 128], F32)
        id_b, ones_b = Buf("id"), Buf("ones")
        ctx.emit(ctx.sp, lambda e: e.dma_start(out=id_sb[:, :], in_=ident[:, :]),
                 writes=[id_b], inc=16)
        ctx.emit(ctx.sp, lambda e: e.dma_start(out=ones_sb[:, :], in_=onesf[:, :]),
                 writes=[ones_b], inc=16)

        with ctx.scope():
            _p0_ternarize(ctx, w1t, w1t_bf, D, 4096, ones_sb, ones_b, half=False, tag="t1")
            _p0_ternarize(ctx, w2t, w2t_bf, 4096, D, ones_sb, ones_b, half=True, tag="t2")

        if "p1" in PHASES_A:
            with ctx.scope():
                _ffn_phase(ctx, xin, xs1 if ("p2" in PHASES_A or "p3" in PHASES_A) else xout,
                           w1t_bf, w2t_bf, b1, b2, id_sb, id_b, ones_sb, ones_b, xln_d,
                           tag="f1")
        if "p2" in PHASES_A:
            src = xs1 if "p1" in PHASES_A else xin
            dst = xs2 if "p3" in PHASES_A else xout
            with ctx.scope():
                _attn_phase(ctx, src, dst, wk, wq, wo, bo, id_sb, id_b, ones_sb, ones_b,
                            xln_d)
        if "p3" in PHASES_A:
            src = xs2 if "p2" in PHASES_A else (xs1 if "p1" in PHASES_A else xin)
            with ctx.scope():
                _ffn_phase(ctx, src, xout, w1t_bf, w2t_bf, b1, b2, id_sb, id_b, ones_sb,
                           ones_b, xln_d, tag="f2")

        ctx.barrier_all()
    return nc


def _p0_ternarize(ctx, wsrc, wdst, rows, cols, ones_sb, ones_b, half, tag):
    nc = ctx.nc
    stack = ctx._stack
    nch = rows // 128
    slab = [ctx.sb(stack, f"{tag}_slab{i}", [128, cols], F32) for i in range(2)]
    slab_b = [Buf(f"{tag}_slab{i}") for i in range(2)]
    absw = ctx.sb(stack, f"{tag}_absw", [128, cols], F32)
    absw_b = Buf(f"{tag}_absw")
    sgn = [ctx.sb(stack, f"{tag}_sgn{i}", [128, cols], F32) for i in range(2)]
    sgn_b = [Buf(f"{tag}_sgn{i}") for i in range(2)]
    mask = ctx.sb(stack, f"{tag}_mask", [128, cols], F32)
    mask_b = Buf(f"{tag}_mask")
    obf = [ctx.sb(stack, f"{tag}_obf{i}", [128, cols], BF16) for i in range(2)]
    obf_b = [Buf(f"{tag}_obf{i}") for i in range(2)]
    red = [ctx.sb(stack, f"{tag}_red{i}", [128, 1], F32) for i in range(2)]
    red_b = [Buf(f"{tag}_red{i}") for i in range(2)]
    part = ctx.sb(stack, f"{tag}_part", [128, 1], F32)
    part_b = Buf(f"{tag}_part")
    tot = ctx.sb(stack, f"{tag}_tot", [1, 1], F32)
    tot_b = Buf(f"{tag}_tot")
    thr = ctx.sb(stack, f"{tag}_thr", [128, 1], F32)
    thr_b = Buf(f"{tag}_thr")
    pss = ctx.psum(stack, f"{tag}_ps", [128, 1], F32)
    pss_b = Buf(f"{tag}_ps")

    ctx.emit(ctx.dve, lambda e: e.memset(part[:, :], 0.0), writes=[part_b])
    for c in range(nch):
        cc = c
        par = c % 2
        q = ctx.sp
        ctx.emit(q, lambda e, cc=cc, par=par: e.dma_start(
            out=slab[par][:, :], in_=wsrc[cc * 128:cc * 128 + 128, :]),
            writes=[slab_b[par]], inc=16)
        ctx.emit(ctx.dve, lambda e, par=par: e.reduce_sum(
            red[par][:, :], slab[par][:, :], axis=AX.X, apply_absolute_value=True),
            reads=[slab_b[par]], writes=[red_b[par]])
        ctx.emit(ctx.dve, lambda e, par=par: e.tensor_tensor(
            part[:, :], part[:, :], red[par][:, :], ALU.add),
            reads=[red_b[par]], writes=[part_b])
    ctx.mm_group([lambda e: e.matmul(pss[0:1, 0:1], part[:, 0:1], ones_sb[:, 0:1],
                                     start=True, stop=True)],
                 reads=[part_b, ones_b], writes=[pss_b])
    numel = rows * cols
    ctx.emit(ctx.act, lambda e: e.activation(tot[:, :], pss[0:1, 0:1], AF.Copy,
                                             scale=0.5 / numel),
             reads=[pss_b], writes=[tot_b])
    ctx.emit(ctx.dve, lambda e: e.tensor_scalar(tot[:, :], tot[:, :], 0.5e-8, None, ALU.max),
             reads=[tot_b], writes=[tot_b])
    ctx.mm_group([lambda e: e.matmul(pss[:, 0:1], ones_sb[0:1, 0:128], tot[0:1, 0:1],
                                     start=True, stop=True)],
                 reads=[tot_b, ones_b], writes=[pss_b])
    ctx.emit(ctx.act, lambda e: e.activation(thr[:, :], pss[:, 0:1], AF.Copy),
             reads=[pss_b], writes=[thr_b])
    for c in range(nch):
        cc = c
        par = c % 2
        ql = ctx.sp
        qs = ctx.pool
        ctx.emit(ql, lambda e, cc=cc, par=par: e.dma_start(
            out=slab[par][:, :], in_=wsrc[cc * 128:cc * 128 + 128, :]),
            writes=[slab_b[par]], inc=16)
        ctx.emit(ctx.act, lambda e, par=par: e.activation(sgn[par][:, :], slab[par][:, :],
                                                          AF.Sign),
                 reads=[slab_b[par]], writes=[sgn_b[par]])
        ctx.emit(ctx.act, lambda e, par=par: e.activation(absw[:, :], slab[par][:, :], AF.Abs),
                 reads=[slab_b[par]], writes=[absw_b])
        ctx.emit(ctx.dve, lambda e: e.tensor_scalar(mask[:, :], absw[:, :], thr[:, :], None,
                                                    ALU.is_ge),
                 reads=[absw_b], sreads=[thr_b], writes=[mask_b])
        if half:
            ctx.emit(ctx.dve, lambda e, par=par: e.tensor_tensor(
                mask[:, :], mask[:, :], sgn[par][:, :], ALU.mult),
                reads=[sgn_b[par]], writes=[mask_b])
            ctx.emit(ctx.dve, lambda e, par=par: e.tensor_scalar(
                obf[par][:, :], mask[:, :], 0.5, None, ALU.mult),
                reads=[mask_b], writes=[obf_b[par]])
        else:
            ctx.emit(ctx.dve, lambda e, par=par: e.tensor_tensor(
                obf[par][:, :], mask[:, :], sgn[par][:, :], ALU.mult),
                reads=[mask_b, sgn_b[par]], writes=[obf_b[par]])
        ctx.emit(qs, lambda e, cc=cc, par=par: e.dma_start(
            out=wdst[cc * 128:cc * 128 + 128, :], in_=obf[par][:, :]),
            reads=[obf_b[par]], writes=[], inc=16)


def _ffn_phase(ctx, xsrc, xdst, w1t_bf, w2t_bf, b1, b2, id_sb, id_b, ones_sb, ones_b,
               xln_d, tag):
    nc = ctx.nc
    stack = ctx._stack
    s_wl = ctx.stream_sem(f"{tag}_wl")
    s_xl = ctx.stream_sem(f"{tag}_xl")
    s_st = ctx.stream_sem(f"{tag}_st")
    wsb1 = ctx.sb(stack, f"{tag}_wsb1", [128, 8, 4096], BF16)
    wsb1_b = Buf(f"{tag}_wsb1")
    wsb2 = ctx.sb(stack, f"{tag}_wsb2", [128, 32, D], BF16)
    wsb2_b = Buf(f"{tag}_wsb2")
    for k in range(8):
        kk = k
        ctx.emit(ctx.sp, lambda e, kk=kk: e.dma_start(out=wsb1[:, kk, :],
                                                      in_=w1t_bf[kk * 128:kk * 128 + 128, :]),
                 writes=[wsb1_b], inc=16, sem=s_wl)
    for fc in range(32):
        ff = fc
        ctx.emit(ctx.sp, lambda e, ff=ff: e.dma_start(out=wsb2[:, ff, :],
                                                      in_=w2t_bf[ff * 128:ff * 128 + 128, :]),
                 writes=[wsb2_b], inc=16, sem=s_wl)

    b1sb = ctx.sb(stack, f"{tag}_b1sb", [128, 32], F32)
    b1sb_b = Buf(f"{tag}_b1sb")
    b2h = ctx.sb(stack, f"{tag}_b2h", [128, D], F32)
    b2h_b = Buf(f"{tag}_b2h")
    psb = ctx.psum(stack, f"{tag}_psb", [128, 512], F32)
    psb_b = Buf(f"{tag}_psb")
    with ctx.scope() as st2:
        b1row = ctx.sb(st2, f"{tag}_b1row", [1, 4096], F32)
        b1row_b = Buf(f"{tag}_b1row")
        b2row = ctx.sb(st2, f"{tag}_b2row", [1, D], F32)
        b2row_b = Buf(f"{tag}_b2row")
        ctx.emit(ctx.sp, lambda e: e.dma_start(out=b1row[:, :], in_=b1[:, :]),
                 writes=[b1row_b], inc=16)
        ctx.emit(ctx.sp, lambda e: e.dma_start(out=b2row[:, :], in_=b2[:, :]),
                 writes=[b2row_b], inc=16)
        for fc in range(32):
            ff = fc
            ctx.mm_group([lambda e, ff=ff: e.matmul(psb[:, 0:1], b1row[0:1, ff * 128:ff * 128 + 128],
                                                    ones_sb[0:1, 0:1], start=True, stop=True)],
                         reads=[b1row_b, ones_b], writes=[psb_b])
            ctx.emit(ctx.dve, lambda e, ff=ff: e.tensor_copy(b1sb[:, ff:ff + 1], psb[:, 0:1]),
                     reads=[psb_b], writes=[b1sb_b])
        _bcast_row_to_partitions(ctx, b2row, b2row_b, ones_sb, ones_b, psb, psb_b,
                                 b2h, b2h_b, D, scale=0.5)

    xlnfm = [ctx.sb(stack, f"{tag}_xlnfm{i}", [128, 8, 512], BF16) for i in range(2)]
    xlnfm_b = [Buf(f"{tag}_xlnfm{i}") for i in range(2)]
    hbf = ctx.sb(stack, f"{tag}_hbf", [128, 32, 512], BF16)
    hbf_b = [Buf(f"{tag}_hbf{fc}") for fc in range(32)]
    xt = [ctx.sb(stack, f"{tag}_xt{i}", [128, D], F32) for i in range(2)]
    xt_b = [Buf(f"{tag}_xt{i}") for i in range(2)]
    xr = xt
    xr_b = xt_b
    xnew = [ctx.sb(stack, f"{tag}_xnew{i}", [128, D], F32) for i in range(2)]
    xnew_b = [Buf(f"{tag}_xnew{i}") for i in range(2)]
    stats = _alloc_ln_slots(ctx, f"{tag}_ln")
    ptr = ctx.psum(stack, f"{tag}_ptr", [128, 8, 128], BF16)
    ptr_b = Buf(f"{tag}_ptr")
    psH = [ctx.psum(stack, f"{tag}_psH{i}", [128, 512], F32) for i in range(2)]
    psH_b = [Buf(f"{tag}_psH{i}") for i in range(2)]
    psO = [ctx.psum(stack, f"{tag}_psO{i}", [128, 512], F32) for i in range(2)]
    psO_b = [Buf(f"{tag}_psO{i}") for i in range(2)]

    ntile = TOK // 512
    s_ln = ctx.stream_sem(f"{tag}_lns")
    for t in range(ntile):
        par = t % 2
        for sub in range(4):
            row0 = t * 512 + sub * 128
            sl = stats[sub % 2]
            ctx.emit(ctx.sp, lambda e, row0=row0, par=par: e.dma_start(
                out=xt[par][:, :], in_=xsrc[row0:row0 + 128, :]),
                writes=[xt_b[par]], inc=16, sem=s_xl)
            _layernorm_subtile(ctx, xt[par], xt_b[par], sl, sub * 128,
                               xlnfm[par], xlnfm_b[par], id_sb, id_b, ptr, ptr_b)
        for k in range(8):
            ctx.emit(ctx.pool, lambda e, k=k, t=t, par=par: e.dma_start(
                out=xln_d[k * 128:k * 128 + 128, t * 512:t * 512 + 512],
                in_=xlnfm[par][:, k, :]),
                reads=[xlnfm_b[par]], inc=16, sem=s_ln)
    for t in range(ntile):
        par = t % 2
        for k in range(8):
            ctx.emit(ctx.sp, lambda e, k=k, t=t, par=par: e.dma_start(
                out=xlnfm[par][:, k, :],
                in_=xln_d[k * 128:k * 128 + 128, t * 512:t * 512 + 512]),
                writes=[xlnfm_b[par]], inc=16, sem=s_ln)
        for fc in range(32):
            ff = fc
            pp = fc % 2
            fns = [
                (lambda e, kk=k, ff=ff, pp=pp, par=par: e.matmul(
                    psH[pp][:, :], wsb1[:, kk, ff * 128:ff * 128 + 128],
                    xlnfm[par][:, kk, :], start=(kk == 0), stop=(kk == 7)))
                for k in range(8)
            ]
            ctx.mm_group(fns, reads=[wsb1_b, xlnfm_b[par]], writes=[psH_b[pp]])
            ctx.emit(ctx.act, lambda e, ff=ff, pp=pp: e.activation(
                hbf[:, ff, :], psH[pp][:, :], AF.Gelu, bias=b1sb[:, ff:ff + 1]),
                reads=[psH_b[pp]], sreads=[b1sb_b], writes=[hbf_b[ff]])
        for sub in range(4):
            row0 = t * 512 + sub * 128
            ctx.emit(ctx.sp, lambda e, row0=row0, par=par: e.dma_start(
                out=xr[par][:, :], in_=xsrc[row0:row0 + 128, :]),
                writes=[xr_b[par]], inc=16, sem=s_xl)
            for half in range(2):
                pp = half
                fns = [
                    (lambda e, ff=fc, sub=sub, half=half, pp=pp, par=par: e.matmul(
                        psO[pp][:, :], hbf[:, ff, sub * 128:sub * 128 + 128],
                        wsb2[:, ff, half * 512:half * 512 + 512],
                        start=(ff == 0), stop=(ff == 31)))
                    for fc in range(32)
                ]
                ctx.mm_group(fns, reads=[wsb2_b] + hbf_b, writes=[psO_b[pp]])
                ctx.emit(ctx.dve, lambda e, half=half, pp=pp, par=par: e.tensor_tensor(
                    xnew[par][:, half * 512:half * 512 + 512], psO[pp][:, :],
                    b2h[:, half * 512:half * 512 + 512], ALU.add),
                    reads=[psO_b[pp], b2h_b], writes=[xnew_b[par]])
            ctx.emit(ctx.dve, lambda e, par=par: e.tensor_tensor(
                xnew[par][:, :], xnew[par][:, :], xr[par][:, :], ALU.add),
                reads=[xr_b[par]], writes=[xnew_b[par]])
            ctx.emit(ctx.pool, lambda e, row0=row0, par=par: e.dma_start(
                out=xdst[row0:row0 + 128, :], in_=xnew[par][:, :]),
                reads=[xnew_b[par]], inc=16, sem=s_st)


def _attn_phase(ctx, xsrc, xdst, wk, wq, wo, bo, id_sb, id_b, ones_sb, ones_b, xln_d):
    nc = ctx.nc
    stack = ctx._stack
    SEQ = 2048

    stats = _alloc_ln_slots(ctx, "a_ln")
    xt = [ctx.sb(stack, f"a_xt{i}", [128, D], F32) for i in range(2)]
    xt_b = [Buf(f"a_xt{i}") for i in range(2)]
    xnew = [ctx.sb(stack, f"a_xnew{i}", [128, D], F32) for i in range(2)]
    xnew_b = [Buf(f"a_xnew{i}") for i in range(2)]
    pso = [ctx.psum(stack, f"a_pso{i}", [65, 512], F32) for i in range(2)]
    pso_b = [Buf(f"a_pso{i}") for i in range(2)]
    prdb = ctx.psum(stack, "a_prdb", [64, 512], F32)
    prdb_b = Buf("a_prdb")
    ebf = [ctx.sb(stack, f"a_ebf{i}", [128, 1024], BF16) for i in range(2)]
    ebf_b = [Buf(f"a_ebf{i}") for i in range(2)]
    osb = [ctx.sb(stack, f"a_osb{i}", [64, 512], F32) for i in range(2)]
    osb_b = [Buf(f"a_osb{i}") for i in range(2)]
    rd = [ctx.sb(stack, f"a_rd{i}", [1, 512], F32) for i in range(2)]
    rd_b = [Buf(f"a_rd{i}") for i in range(2)]
    s_xl = ctx.stream_sem("a_xl")
    s_wl = ctx.stream_sem("a_wl")
    s_st = ctx.stream_sem("a_st")

    for bseq in range(2):
        with ctx.scope() as bstk:
            attn = ctx.sb(bstk, f"a_attn{bseq}", [128, 8, SEQ], BF16)
            attn_b = Buf(f"a_attn{bseq}")
            with ctx.scope() as s2:
                qfm = ctx.sb(s2, f"a_qfm{bseq}", [128, 8, SEQ], BF16)
                qfm_b = Buf(f"a_qfm{bseq}")
                kfm = ctx.sb(s2, f"a_kfm{bseq}", [128, 8, SEQ], BF16)
                kfm_b = Buf(f"a_kfm{bseq}")
                ktm = ctx.sb(s2, f"a_ktm{bseq}", [128, 16, 1040], BF16)
                ktm_b = [Buf(f"a_ktm{i}") for i in range(16)]
                ones_view = bass.AP(ktm[:, :, :].tensor, 64,
                                    [[16 * 1040, 128], [1040, 16], [65, 16], [1, 1]])
                ctx.emit(ctx.dve, lambda e: e.memset(ones_view, 1.0), writes=ktm_b)
                with ctx.scope() as s1:
                    ptr = ctx.psum(s1, f"a_ptr{bseq}", [128, 8, 128], BF16)
                    ptr_b = Buf("a_ptr")
                    psA = [ctx.psum(s1, f"a_psA{bseq}{i}", [128, 512], F32) for i in range(2)]
                    psA_b = [Buf(f"a_psA{i}") for i in range(2)]
                    wqsb = ctx.sb(s1, f"a_wqsb{bseq}", [128, 8, D], BF16)
                    wqsb_b = Buf(f"a_wqsb{bseq}")
                    wksb = ctx.sb(s1, f"a_wksb{bseq}", [128, 8, D], BF16)
                    wksb_b = Buf(f"a_wksb{bseq}")
                    xlnfm = ctx.sb(s1, f"a_xlnfm{bseq}", [128, 8, 512], BF16)
                    xlnfm_b = Buf(f"a_xlnfm{bseq}")
                    ctx.emit(ctx.pool, lambda e: e.dma_start(
                        out=wqsb[:, :, :], in_=wq.ap().rearrange("(c p) n -> p c n", p=128)),
                        writes=[wqsb_b], inc=16, sem=s_wl)
                    ctx.emit(ctx.pool, lambda e: e.dma_start(
                        out=wksb[:, :, :], in_=wk.ap().rearrange("(c p) n -> p c n", p=128)),
                        writes=[wksb_b], inc=16, sem=s_wl)
                    s_ln = ctx.stream_sem(f"a_lns{bseq}")
                    for tg in range(4):
                        for sub in range(4):
                            row0 = bseq * SEQ + tg * 512 + sub * 128
                            par = sub % 2
                            ctx.emit(ctx.sp, lambda e, row0=row0, par=par: e.dma_start(
                                out=xt[par][:, :], in_=xsrc[row0:row0 + 128, :]),
                                writes=[xt_b[par]], inc=16, sem=s_xl)
                            _layernorm_subtile(ctx, xt[par], xt_b[par], stats[sub % 2], sub * 128,
                                               xlnfm, xlnfm_b, id_sb, id_b, ptr, ptr_b)
                        for k in range(8):
                            ctx.emit(ctx.pool, lambda e, k=k, tg=tg: e.dma_start(
                                out=xln_d[k * 128:k * 128 + 128, tg * 512:tg * 512 + 512],
                                in_=xlnfm[:, k, :]),
                                reads=[xlnfm_b], inc=16, sem=s_ln)
                    for tg in range(4):
                        for k in range(8):
                            ctx.emit(ctx.sp, lambda e, k=k, tg=tg: e.dma_start(
                                out=xlnfm[:, k, :],
                                in_=xln_d[k * 128:k * 128 + 128, tg * 512:tg * 512 + 512]),
                                writes=[xlnfm_b], inc=16, sem=s_ln)
                        for (wsb, wsb_b, dstt, dst_b) in ((wqsb, wqsb_b, qfm, qfm_b),
                                                          (wksb, wksb_b, kfm, kfm_b)):
                            for dc in range(8):
                                pp = dc % 2
                                fns = [
                                    (lambda e, kk=k, dc=dc, pp=pp: e.matmul(
                                        psA[pp][:, :], wsb[:, kk, dc * 128:dc * 128 + 128],
                                        xlnfm[:, kk, :], start=(kk == 0), stop=(kk == 7)))
                                    for k in range(8)
                                ]
                                ctx.mm_group(fns, reads=[wsb_b, xlnfm_b], writes=[psA_b[pp]])
                                ctx.emit(ctx.act, lambda e, dc=dc, tg=tg, pp=pp: e.activation(
                                    dstt[:, dc, tg * 512:tg * 512 + 512], psA[pp][:, :], AF.Copy),
                                    reads=[psA_b[pp]], writes=[dst_b])
                    for kt in range(16):
                        fns = [
                            (lambda e, dc=dc, kt=kt: e.transpose(
                                ptr[:, dc, :], kfm[:, dc, kt * 128:kt * 128 + 128], id_sb[:, :]))
                            for dc in range(8)
                        ]
                        ctx.mm_group(fns, reads=[kfm_b, id_b], writes=[ptr_b])
                        for dc in range(8):
                            dst = bass.AP(ktm[:, :, :].tensor, kt * 1040 + dc * 130,
                                          [[16 * 1040, 128], [65, 2], [1, 64]])
                            srcv = ptr[:, dc, :].rearrange("p (a b) -> p a b", b=64)
                            ctx.emit(ctx.dve, lambda e, dst=dst, srcv=srcv: e.tensor_copy(dst, srcv),
                                     reads=[ptr_b], writes=[ktm_b[kt]])
                psS = [ctx.psum(s2, f"a_psS{bseq}{i}", [128, 1024], F32) for i in range(2)]
                psS_b = [Buf(f"a_psS{i}") for i in range(2)]
                for h in range(16):
                    pb = 64 * (h & 1)
                    chq = h // 2
                    for qq in range(4):
                        po = qq % 2
                        for kp in range(8):
                            j = kp % 2
                            t0, t1 = 2 * kp, 2 * kp + 1
                            ctx.mm_group([
                                (lambda e, qq=qq, t0=t0, j=j, pb=pb, chq=chq: e.matmul(
                                    psS[j][:, 0:512],
                                    kfm[pb:pb + 64, chq, t0 * 128:t0 * 128 + 128],
                                    qfm[pb:pb + 64, chq, qq * 512:qq * 512 + 512],
                                    start=True, stop=True)),
                                (lambda e, qq=qq, t1=t1, j=j, pb=pb, chq=chq: e.matmul(
                                    psS[j][:, 512:1024],
                                    kfm[pb:pb + 64, chq, t1 * 128:t1 * 128 + 128],
                                    qfm[pb:pb + 64, chq, qq * 512:qq * 512 + 512],
                                    start=True, stop=True))],
                                reads=[kfm_b, qfm_b], writes=[psS_b[j]])
                            ctx.emit(ctx.act, lambda e, j=j: e.activation(
                                ebf[j][:, :], psS[j][:, :], AF.Exp, scale=1.0 / 32.0),
                                reads=[psS_b[j]], writes=[ebf_b[j]])
                            ctx.mm_group([
                                (lambda e, h=h, t0=t0, j=j, kp=kp, po=po: e.matmul(
                                    pso[po][:, :], ktm[:, t0, 65 * h:65 * h + 65],
                                    ebf[j][:, 0:512], start=(kp == 0), stop=False,
                                    skip_group_check=True)),
                                (lambda e, h=h, t1=t1, j=j, kp=kp, po=po: e.matmul(
                                    pso[po][:, :], ktm[:, t1, 65 * h:65 * h + 65],
                                    ebf[j][:, 512:1024], start=False, stop=(kp == 7),
                                    skip_group_check=True))],
                                reads=[ktm_b[t0], ktm_b[t1], ebf_b[j]], writes=[pso_b[po]])
                        ctx.emit(ctx.dve, lambda e, po=po: e.reciprocal(
                            rd[po][:, :], pso[po][64:65, :]),
                            reads=[pso_b[po]], writes=[rd_b[po]])
                        ctx.mm_group([lambda e, po=po: e.matmul(
                            prdb[:, :], ones_sb[0:1, 0:64], rd[po][0:1, :],
                            start=True, stop=True)],
                            reads=[rd_b[po], ones_b], writes=[prdb_b])
                        ctx.emit(ctx.act, lambda e, po=po: e.activation(
                            osb[po][:, :], pso[po][0:64, :], AF.Copy),
                            reads=[pso_b[po]], writes=[osb_b[po]])
                        ctx.emit(ctx.dve, lambda e, h=h, qq=qq, po=po, pb=pb, chq=chq:
                                 e.tensor_tensor(
                                     attn[pb:pb + 64, chq, qq * 512:qq * 512 + 512],
                                     osb[po][:, :], prdb[:, :], ALU.mult),
                                 reads=[osb_b[po], prdb_b], writes=[attn_b])
            with ctx.scope() as s3:
                psE = [ctx.psum(s3, f"a_psE{bseq}{i}", [128, 512], F32) for i in range(2)]
                psE_b = [Buf(f"a_psE{i}") for i in range(2)]
                wosb = ctx.sb(s3, f"a_wosb{bseq}", [128, 8, D], BF16)
                wosb_b = Buf(f"a_wosb{bseq}")
                bobc = ctx.sb(s3, f"a_bobc{bseq}", [128, D], F32)
                bobc_b = Buf(f"a_bobc{bseq}")
                ctx.emit(ctx.pool, lambda e: e.dma_start(
                    out=wosb[:, :, :], in_=wo.ap().rearrange("(c p) n -> p c n", p=128)),
                    writes=[wosb_b], inc=16, sem=s_wl)
                with ctx.scope() as s4:
                    borow = ctx.sb(s4, f"a_borow{bseq}", [1, D], F32)
                    borow_b = Buf(f"a_borow{bseq}")
                    ctx.emit(ctx.sp, lambda e: e.dma_start(out=borow[:, :], in_=bo[:, :]),
                             writes=[borow_b], inc=16)
                    _bcast_row_to_partitions(ctx, borow, borow_b, ones_sb, ones_b,
                                             psE[0], psE_b[0], bobc, bobc_b, D)
                for sub in range(16):
                    row0 = bseq * SEQ + sub * 128
                    par = sub % 2
                    ctx.emit(ctx.sp, lambda e, row0=row0, par=par: e.dma_start(
                        out=xt[par][:, :], in_=xsrc[row0:row0 + 128, :]),
                        writes=[xt_b[par]], inc=16, sem=s_xl)
                    for half in range(2):
                        pp = half
                        fns = [
                            (lambda e, c=c, sub=sub, half=half, pp=pp: e.matmul(
                                psE[pp][:, :], attn[:, c, sub * 128:sub * 128 + 128],
                                wosb[:, c, half * 512:half * 512 + 512],
                                start=(c == 0), stop=(c == 7)))
                            for c in range(8)
                        ]
                        ctx.mm_group(fns, reads=[attn_b, wosb_b], writes=[psE_b[pp]])
                        ctx.emit(ctx.dve, lambda e, half=half, pp=pp, par=par: e.tensor_tensor(
                            xnew[par][:, half * 512:half * 512 + 512], psE[pp][:, :],
                            bobc[:, half * 512:half * 512 + 512], ALU.add),
                            reads=[psE_b[pp], bobc_b], writes=[xnew_b[par]])
                    ctx.emit(ctx.dve, lambda e, par=par: e.tensor_tensor(
                        xnew[par][:, :], xnew[par][:, :], xt[par][:, :], ALU.add),
                        reads=[xt_b[par]], writes=[xnew_b[par]])
                    ctx.emit(ctx.pool, lambda e, row0=row0, par=par: e.dma_start(
                        out=xdst[row0:row0 + 128, :], in_=xnew[par][:, :]),
                        reads=[xnew_b[par]], inc=16, sem=s_st)


def build_launch_B():
    nc = bass.Bass("TRN2", target_bir_lowering=False, debug=False)

    xin = nc.dram_tensor("xin", [TOK, D], F32, kind="ExternalInput")
    wgm = nc.dram_tensor("wgm", [D, 8], F32, kind="ExternalInput")
    w1m = nc.dram_tensor("w1m", [8 * 8 * D, 256], F32, kind="ExternalInput")
    w2m = nc.dram_tensor("w2m", [8 * 8 * D, 256], F32, kind="ExternalInput")
    w3m = nc.dram_tensor("w3m", [8 * 2048, D], F32, kind="ExternalInput")
    convw = nc.dram_tensor("convw", [D, 4], F32, kind="ExternalInput")
    wq1 = nc.dram_tensor("wq1", [D, 512], F32, kind="ExternalInput")
    wkg = nc.dram_tensor("wkg", [D, 512], F32, kind="ExternalInput")
    wv1 = nc.dram_tensor("wv1", [D, D], F32, kind="ExternalInput")
    wg1 = nc.dram_tensor("wg1", [D, D], F32, kind="ExternalInput")
    bg1 = nc.dram_tensor("bg1", [1, D], F32, kind="ExternalInput")
    wout = nc.dram_tensor("wout", [D, D], F32, kind="ExternalInput")
    augb = nc.dram_tensor("augb", [1, 512], F32, kind="ExternalInput")
    ident = nc.dram_tensor("ident", [128, 128], BF16, kind="ExternalInput")
    onesf = nc.dram_tensor("onesf", [128, 128], F32, kind="ExternalInput")
    onesb = nc.dram_tensor("onesb", [128, 128], BF16, kind="ExternalInput")
    identf = nc.dram_tensor("identf", [128, 128], F32, kind="ExternalInput")
    eqk = nc.dram_tensor("eqk", [2 * 128, 8], BF16, kind="ExternalInput")
    ebcA = nc.dram_tensor("ebcA", [4 * 8, 128], BF16, kind="ExternalInput")
    esumV = nc.dram_tensor("esumV", [4 * 128, 8], F32, kind="ExternalInput")
    ebcVf = nc.dram_tensor("ebcVf", [4 * 8, 128], F32, kind="ExternalInput")

    xout = nc.dram_tensor("xout", [TOK, D], F32, kind="ExternalOutput")

    xs1 = nc.dram_tensor("xs1", [TOK, D], F32)
    xs2 = nc.dram_tensor("xs2", [TOK, D], F32)
    xcfm_d = nc.dram_tensor("xcfm_d", [D, TOK], BF16)
    gsil_d = nc.dram_tensor("gsil_d", [D, TOK], BF16)

    ctx = Ctx(nc)
    with nc.Block():
        id_sb = nc.alloc_sbuf_tensor("id_sb", [128, 128], BF16)
        ones_sb = nc.alloc_sbuf_tensor("ones_sb", [128, 128], F32)
        id_b, ones_b = Buf("id"), Buf("ones")
        ctx.emit(ctx.sp, lambda e: e.dma_start(out=id_sb[:, :], in_=ident[:, :]),
                 writes=[id_b], inc=16)
        ctx.emit(ctx.sp, lambda e: e.dma_start(out=ones_sb[:, :], in_=onesf[:, :]),
                 writes=[ones_b], inc=16)
        onesbf_sb = nc.alloc_sbuf_tensor("onesbf_sb", [128, 128], BF16)
        onesbf_b = Buf("onesbf")
        ctx.emit(ctx.sp, lambda e: e.dma_start(out=onesbf_sb[:, :], in_=onesb[:, :]),
                 writes=[onesbf_b], inc=16)
        idf_sb = nc.alloc_sbuf_tensor("idf_sb", [128, 128], F32)
        idf_b = Buf("idf")
        ctx.emit(ctx.sp, lambda e: e.dma_start(out=idf_sb[:, :], in_=identf[:, :]),
                 writes=[idf_b], inc=16)
        cst = dict(id_sb=id_sb, id_b=id_b, ones_sb=ones_sb, ones_b=ones_b,
                   onesbf_sb=onesbf_sb, onesbf_b=onesbf_b, idf_sb=idf_sb, idf_b=idf_b)

        if "p4" in PHASES_B:
            dst = xs1 if ("p5" in PHASES_B or "p6" in PHASES_B) else xout
            with ctx.scope():
                _moe_phase(ctx, xin, dst, wgm, w1m, w2m, w3m, cst, tag="m1")
        if "p5" in PHASES_B:
            src = xs1 if "p4" in PHASES_B else xin
            dst = xs2 if "p6" in PHASES_B else xout
            with ctx.scope():
                _gla_phase(ctx, src, dst, convw, wq1, wkg, wv1, wg1, bg1, wout, augb,
                           eqk, ebcA, esumV, ebcVf, xcfm_d, gsil_d, cst)
        if "p6" in PHASES_B:
            src = xs2 if "p5" in PHASES_B else (xs1 if "p4" in PHASES_B else xin)
            with ctx.scope():
                _moe_phase(ctx, src, xout, wgm, w1m, w2m, w3m, cst, tag="m2")
        ctx.barrier_all()
    return nc


def _moe_phase(ctx, xsrc, xdst, wgm, w1m, w2m, w3m, cst, tag):
    nc = ctx.nc
    stack = ctx._stack
    id_sb, id_b = cst["id_sb"], cst["id_b"]
    ones_sb, ones_b = cst["ones_sb"], cst["ones_b"]
    onesbf_sb, onesbf_b = cst["onesbf_sb"], cst["onesbf_b"]

    s_wl = ctx.stream_sem(f"{tag}_wl")
    s_st = ctx.stream_sem(f"{tag}_st")
    s_xl = ctx.stream_sem(f"{tag}_xl")
    xlnfm = ctx.sb(stack, f"{tag}_xlnfm", [128, 8, TOK], BF16)
    xlnfm_bs = [Buf(f"{tag}_xlnfm{g}") for g in range(NSUB // 4)]
    gv = ctx.sb(stack, f"{tag}_gv", [128, 32, 8], F32)
    gv_b = Buf(f"{tag}_gv")

    xdst_b = [Buf(f"{tag}_xdst{i}") for i in range(32)]

    s_wl3 = ctx.stream_sem(f"{tag}_wl3")
    w1sb = ctx.sb(stack, f"{tag}_w1sb", [128, 8, 2048], BF16)
    w1sb_b = [Buf(f"{tag}_w1sb{c}") for c in range(8)]
    w2sb = ctx.sb(stack, f"{tag}_w2sb", [128, 8, 2048], BF16)
    w2sb_b = [Buf(f"{tag}_w2sb{c}") for c in range(8)]
    w3sb = ctx.sb(stack, f"{tag}_w3sb", [128, 16, D], BF16)
    w3sb_b = [Buf(f"{tag}_w3sb{h}") for h in range(2)]

    def _load_w12_chunk(ex, c8):
        for (wsb, wsb_b, wm) in ((w1sb, w1sb_b, w1m), (w2sb, w2sb_b, w2m)):
            r0 = (ex * 8 + c8) * D
            ctx.emit(ctx.pool, lambda e, wsb=wsb, wm=wm, r0=r0, c8=c8: e.dma_start(
                out=wsb[:, :, c8 * 256:c8 * 256 + 256],
                in_=wm[r0:r0 + D, :].rearrange("(c p) n -> p c n", p=128)),
                writes=[wsb_b[c8]], inc=16, sem=s_wl)

    wstg = [ctx.sb(stack, f"{tag}_wstg{i}", [128, 4, 256], F32) for i in range(2)]
    wstg_b = [Buf(f"{tag}_wstg{i}") for i in range(2)]

    def _prefetch_w12_chunk(ex, c8):
        for (wsb, wsb_b, wm) in ((w1sb, w1sb_b, w1m), (w2sb, w2sb_b, w2m)):
            r0 = (ex * 8 + c8) * D
            for hh in range(2):
                ctx.emit(ctx.sp, lambda e, wm=wm, r0=r0, hh=hh: e.dma_start(
                    out=wstg[hh][:, :, :],
                    in_=wm[r0 + hh * 512:r0 + hh * 512 + 512, :]
                        .rearrange("(c p) n -> p c n", p=128)),
                    writes=[wstg_b[hh]], inc=16, sem=s_wl)
                ctx.emit(ctx.act, lambda e, wsb=wsb, c8=c8, hh=hh: e.activation(
                    wsb[:, hh * 4:hh * 4 + 4, c8 * 256:c8 * 256 + 256],
                    wstg[hh][:, :, :], AF.Copy),
                    reads=[wstg_b[hh]], writes=[wsb_b[c8]])

    def _load_w3(ex):
        for hh in range(2):
            ctx.emit(ctx.pool, lambda e, ex=ex, hh=hh: e.dma_start(
                out=w3sb[:, hh * 8:hh * 8 + 8, :],
                in_=w3m[ex * 2048 + hh * 1024:ex * 2048 + hh * 1024 + 1024, :]
                    .rearrange("(c p) n -> p c n", p=128)),
                writes=[w3sb_b[hh]], inc=16, sem=s_wl3)

    with ctx.scope() as s1:
        stats = _alloc_ln_slots(ctx, f"{tag}_ln")
        xt = [ctx.sb(s1, f"{tag}_xt{i}", [128, D], F32) for i in range(2)]
        xt_b = [Buf(f"{tag}_xt{i}") for i in range(2)]
        ptr = ctx.psum(s1, f"{tag}_ptr", [128, 8, 128], BF16)
        ptr_b = Buf(f"{tag}_ptr")
        wgsb = ctx.sb(s1, f"{tag}_wgsb", [128, 8, 8], BF16)
        wgsb_b = Buf(f"{tag}_wgsb")
        ctx.emit(ctx.pool, lambda e: e.dma_start(
            out=wgsb[:, :, :], in_=wgm.ap().rearrange("(c p) n -> p c n", p=128)),
            writes=[wgsb_b], inc=16)
        for c8 in range(8):
            _load_w12_chunk(0, c8)
        _load_w3(0)
        gs = ctx.sb(s1, f"{tag}_gs", [128, 32, 8], F32)
        gs_b = Buf(f"{tag}_gs")
        psG = ctx.psum(s1, f"{tag}_psG", [128, 8], F32)
        psG_b = Buf(f"{tag}_psG")
        for sub in range(NSUB):
            r0 = sub * 128
            par = sub % 2
            ctx.emit(ctx.sp, lambda e, r0=r0, par=par: e.dma_start(
                out=xt[par][:, :], in_=xsrc[r0:r0 + 128, :]),
                writes=[xt_b[par]], inc=16)
            _layernorm_subtile(ctx, xt[par], xt_b[par], stats[sub % 2], sub * 128,
                               xlnfm, xlnfm_bs[sub // 4], id_sb, id_b, ptr, ptr_b)
        for sub in range(NSUB):
            fns = [
                (lambda e, kk=k, sub=sub: e.matmul(
                    psG[:, :], xlnfm[:, kk, sub * 128:sub * 128 + 128],
                    wgsb[:, kk, :], start=(kk == 0), stop=(kk == 7)))
                for k in range(8)
            ]
            ctx.mm_group(fns, reads=[xlnfm_bs[sub // 4], wgsb_b], writes=[psG_b])
            ctx.emit(ctx.act, lambda e, sub=sub: e.activation(
                gs[:, sub, :], psG[:, :], AF.Copy),
                reads=[psG_b], writes=[gs_b])
        m1 = ctx.sb(s1, f"{tag}_m1", [128, 8, 1], F32)
        m1_b = Buf(f"{tag}_m1")
        m2 = ctx.sb(s1, f"{tag}_m2", [128, 8, 1], F32)
        m2_b = Buf(f"{tag}_m2")
        mk1 = ctx.sb(s1, f"{tag}_mk1", [128, 8, 8], F32)
        mk1_b = Buf(f"{tag}_mk1")
        mk2 = ctx.sb(s1, f"{tag}_mk2", [128, 8, 8], F32)
        mk2_b = Buf(f"{tag}_mk2")
        gs2 = ctx.sb(s1, f"{tag}_gs2", [128, 8, 8], F32)
        gs2_b = Buf(f"{tag}_gs2")
        dd = ctx.sb(s1, f"{tag}_dd", [128, 8, 1], F32)
        dd_b = Buf(f"{tag}_dd")
        p2v = ctx.sb(s1, f"{tag}_p2", [128, 8, 1], F32)
        p2v_b = Buf(f"{tag}_p2")
        p1h = ctx.sb(s1, f"{tag}_p1h", [128, 8, 1], F32)
        p1h_b = Buf(f"{tag}_p1h")
        p2h = ctx.sb(s1, f"{tag}_p2h", [128, 8, 1], F32)
        p2h_b = Buf(f"{tag}_p2h")
        def bview(t):
            a = t[:, :, 0:1]
            return bass.AP(a.tensor, a.offset, [list(q) for q in a.ap[:-1]] + [[0, 8]])

        for g8 in range(4):
            sl = gs[:, g8 * 8:(g8 + 1) * 8, :]
            ctx.emit(ctx.dve, lambda e, sl=sl: e.reduce_max(m1[:, :, :], sl, axis=AX.X),
                     reads=[gs_b], writes=[m1_b])
            ctx.emit(ctx.dve, lambda e, sl=sl: e.tensor_tensor(
                mk1[:, :, :], sl, bview(m1), ALU.is_ge),
                reads=[gs_b, m1_b], writes=[mk1_b])
            ctx.emit(ctx.dve, lambda e, sl=sl: e.scalar_tensor_tensor(
                gs2[:, :, :], mk1[:, :, :], -1e9, sl, ALU.mult, ALU.add),
                reads=[mk1_b, gs_b], writes=[gs2_b])
            ctx.emit(ctx.dve, lambda e: e.reduce_max(m2[:, :, :], gs2[:, :, :], axis=AX.X),
                     reads=[gs2_b], writes=[m2_b])
            ctx.emit(ctx.dve, lambda e: e.tensor_tensor(
                mk2[:, :, :], gs2[:, :, :], bview(m2), ALU.is_ge),
                reads=[gs2_b, m2_b], writes=[mk2_b])
            ctx.emit(ctx.dve, lambda e: e.tensor_tensor(
                dd[:, :, :], m2[:, :, :], m1[:, :, :], ALU.subtract),
                reads=[m2_b, m1_b], writes=[dd_b])
            ctx.emit(ctx.act, lambda e: e.activation(p2v[:, :, :], dd[:, :, :], AF.Sigmoid),
                     reads=[dd_b], writes=[p2v_b])
            ctx.emit(ctx.dve, lambda e: e.tensor_scalar(
                p1h[:, :, :], p2v[:, :, :], -0.5, 0.5, ALU.mult, ALU.add),
                reads=[p2v_b], writes=[p1h_b])
            ctx.emit(ctx.dve, lambda e: e.tensor_scalar(
                p2h[:, :, :], p2v[:, :, :], 0.5, None, ALU.mult),
                reads=[p2v_b], writes=[p2h_b])
            gvs = gv[:, g8 * 8:(g8 + 1) * 8, :]
            ctx.emit(ctx.dve, lambda e, gvs=gvs: e.tensor_tensor(
                gvs, mk1[:, :, :], bview(p1h), ALU.mult),
                reads=[mk1_b, p1h_b], writes=[gv_b])
            ctx.emit(ctx.dve, lambda e: e.tensor_tensor(
                mk2[:, :, :], mk2[:, :, :], bview(p2h), ALU.mult),
                reads=[p2h_b], writes=[mk2_b])
            ctx.emit(ctx.dve, lambda e, gvs=gvs: e.tensor_tensor(
                gvs, gvs, mk2[:, :, :], ALU.add),
                reads=[mk2_b], writes=[gv_b])

    hbf = ctx.sb(stack, f"{tag}_hbf", [128, 16, 512], BF16)
    hbf_b = [Buf(f"{tag}_hbf{fc}") for fc in range(16)]
    s1f = [ctx.sb(stack, f"{tag}_s1f{i}", [128, 512], F32) for i in range(2)]
    s1f_b = [Buf(f"{tag}_s1f{i}") for i in range(2)]
    obuf = [ctx.sb(stack, f"{tag}_obuf{i}", [128, D], F32) for i in range(2)]
    obuf_b = [Buf(f"{tag}_obuf{i}") for i in range(2)]
    xcur = [ctx.sb(stack, f"{tag}_xcur{i}", [128, D], F32) for i in range(2)]
    xcur_b = [Buf(f"{tag}_xcur{i}") for i in range(2)]
    psH1 = [ctx.psum(stack, f"{tag}_psH1{i}", [128, 512], F32) for i in range(3)]
    psH1_b = [Buf(f"{tag}_psH1{i}") for i in range(3)]
    psH2 = [ctx.psum(stack, f"{tag}_psH2{i}", [128, 512], F32) for i in range(3)]
    psH2_b = [Buf(f"{tag}_psH2{i}") for i in range(3)]
    psO = [ctx.psum(stack, f"{tag}_psO{i}", [128, 512], F32) for i in range(2)]
    psO_b = [Buf(f"{tag}_psO{i}") for i in range(2)]

    for ex in range(MOE_EXPERTS):
        for ts in range(8):
            c0 = ts * 512
            for fc in range(16):
                pp = fc % 3
                fns = [
                    (lambda e, kk=k, fc=fc, pp=pp, c0=c0: e.matmul(
                        psH1[pp][:, :], w1sb[:, kk, fc * 128:fc * 128 + 128],
                        xlnfm[:, kk, c0:c0 + 512], start=(kk == 0), stop=(kk == 7)))
                    for k in range(8)
                ]
                ctx.mm_group(fns, reads=[w1sb_b[fc // 2], xlnfm_bs[ts]], writes=[psH1_b[pp]])
                fns = [
                    (lambda e, kk=k, fc=fc, pp=pp, c0=c0: e.matmul(
                        psH2[pp][:, :], w2sb[:, kk, fc * 128:fc * 128 + 128],
                        xlnfm[:, kk, c0:c0 + 512], start=(kk == 0), stop=(kk == 7)))
                    for k in range(8)
                ]
                ctx.mm_group(fns, reads=[w2sb_b[fc // 2], xlnfm_bs[ts]], writes=[psH2_b[pp]])
                if ts == 7 and fc % 2 == 1 and ex + 1 < MOE_EXPERTS:
                    _prefetch_w12_chunk(ex + 1, fc // 2)
                sp2 = fc % 2
                ctx.emit(ctx.act, lambda e, pp=pp, sp2=sp2: e.activation(
                    s1f[sp2][:, :], psH1[pp][:, :], AF.Silu),
                    reads=[psH1_b[pp]], writes=[s1f_b[sp2]])
                ctx.emit(ctx.dve, lambda e, pp=pp, fc=fc, sp2=sp2: e.tensor_tensor(
                    hbf[:, fc, :], s1f[sp2][:, :], psH2[pp][:, :], ALU.mult),
                    reads=[s1f_b[sp2], psH2_b[pp]], writes=[hbf_b[fc]])
            for sub4 in range(4):
                sub = ts * 4 + sub4
                par = sub % 2
                r0 = sub * 128
                acc_src = xsrc if ex == 0 else xdst
                ctx.emit(ctx.sp, lambda e, r0=r0, par=par, acc_src=acc_src: e.dma_start(
                    out=xcur[par][:, :], in_=acc_src[r0:r0 + 128, :]),
                    reads=[xdst_b[sub]] if ex > 0 else [],
                    writes=[xcur_b[par]], inc=16, sem=s_xl)
                for dh in range(2):
                    pp = dh
                    fns = [
                        (lambda e, fc=fc, sub4=sub4, dh=dh, pp=pp: e.matmul(
                            psO[pp][:, :], hbf[:, fc, sub4 * 128:sub4 * 128 + 128],
                            w3sb[:, fc, dh * 512:dh * 512 + 512],
                            start=(fc == 0), stop=(fc == 15)))
                        for fc in range(16)
                    ]
                    ctx.mm_group(fns, reads=hbf_b + w3sb_b, writes=[psO_b[pp]])
                    ctx.emit(ctx.act, lambda e, dh=dh, pp=pp, par=par, ex=ex, sub=sub:
                             e.activation(obuf[par][:, dh * 512:dh * 512 + 512],
                                          psO[pp][:, :], AF.Identity,
                                          scale=gv[:, sub, ex:ex + 1]),
                             reads=[psO_b[pp]], sreads=[gv_b], writes=[obuf_b[par]])
                ctx.emit(ctx.dve, lambda e, par=par: e.tensor_tensor(
                    obuf[par][:, :], obuf[par][:, :], xcur[par][:, :], ALU.add),
                    reads=[xcur_b[par]], writes=[obuf_b[par]])
                ctx.emit(ctx.pool, lambda e, r0=r0, par=par: e.dma_start(
                    out=xdst[r0:r0 + 128, :], in_=obuf[par][:, :]),
                    reads=[obuf_b[par]], writes=[xdst_b[sub]], inc=16, sem=s_st)
        if ex + 1 < MOE_EXPERTS:
            _load_w3(ex + 1)


def _gla_phase(ctx, xsrc, xdst, convw, wq1, wkg, wv1, wg1, bg1, wout, augb,
               eqk, ebcA, esumV, ebcVf, xcfm_d, gsil_d, cst):
    nc = ctx.nc
    stack = ctx._stack
    id_sb, id_b = cst["id_sb"], cst["id_b"]
    ones_sb, ones_b = cst["ones_sb"], cst["ones_b"]
    RSQ32 = 0.17677669529663687
    s_xl = ctx.stream_sem("g_xl")
    s_wl = ctx.stream_sem("g_wl")
    s_st = ctx.stream_sem("g_st")
    s_sc = ctx.stream_sem("g_sc")

    on_d = nc.dram_tensor("gla_on_d", [D, TOK], BF16)

    eqk_sb = ctx.sb(stack, "g_eqk", [128, 2, 8], BF16)
    eqk_b = Buf("g_eqk")
    ctx.emit(ctx.sp, lambda e: e.dma_start(
        out=eqk_sb[:, :, :], in_=eqk.ap().rearrange("(c p) n -> p c n", p=128)),
        writes=[eqk_b], inc=16)
    ebcA_sb = ctx.sb(stack, "g_ebcA", [8, 4, 128], BF16)
    ebcA_b = Buf("g_ebcA")
    ctx.emit(ctx.sp, lambda e: e.dma_start(
        out=ebcA_sb[:, :, :], in_=ebcA.ap().rearrange("(c p) n -> p c n", p=8)),
        writes=[ebcA_b], inc=16)
    esumV_sb = ctx.sb(stack, "g_esumV", [128, 4, 8], F32)
    esumV_b = Buf("g_esumV")
    ctx.emit(ctx.sp, lambda e: e.dma_start(
        out=esumV_sb[:, :, :], in_=esumV.ap().rearrange("(c p) n -> p c n", p=128)),
        writes=[esumV_b], inc=16)
    ebcVf_sb = ctx.sb(stack, "g_ebcVf", [8, 4, 128], F32)
    ebcVf_b = Buf("g_ebcVf")
    ctx.emit(ctx.sp, lambda e: e.dma_start(
        out=ebcVf_sb[:, :, :], in_=ebcVf.ap().rearrange("(c p) n -> p c n", p=8)),
        writes=[ebcVf_b], inc=16)

    with ctx.scope() as s0:
        stats = _alloc_ln_slots(ctx, "g_ln")
        xt = [ctx.sb(s0, f"g_xt{i}", [128, D], F32) for i in range(2)]
        xt_b = [Buf(f"g_xt{i}") for i in range(2)]
        ptr = ctx.psum(s0, "g_ptr", [128, 8, 128], BF16)
        ptr_b = Buf("g_ptr")
        xlnfm = ctx.sb(s0, "g_xlnfm", [128, 8, TOK], BF16)
        xlnfm_b = Buf("g_xlnfm")
        for sub in range(NSUB):
            r0 = sub * 128
            par = sub % 2
            ctx.emit(ctx.sp, lambda e, r0=r0, par=par: e.dma_start(
                out=xt[par][:, :], in_=xsrc[r0:r0 + 128, :]),
                writes=[xt_b[par]], inc=16, sem=s_xl)
            _layernorm_subtile(ctx, xt[par], xt_b[par], stats[sub % 2], sub * 128,
                               xlnfm, xlnfm_b, id_sb, id_b, ptr, ptr_b)
        convsb = ctx.sb(s0, "g_convsb", [128, 8, 4], F32)
        convsb_b = Buf("g_convsb")
        ctx.emit(ctx.sp, lambda e: e.dma_start(
            out=convsb[:, :, :], in_=convw.ap().rearrange("(c p) n -> p c n", p=128)),
            writes=[convsb_b], inc=16)
        yc = ctx.sb(s0, "g_yc", [128, TOK], F32)
        yc_b = Buf("g_yc")
        xcbf = ctx.sb(s0, "g_xcbf", [128, TOK], BF16)
        xcbf_b = Buf("g_xcbf")
        for c in range(8):
            ctx.emit(ctx.act, lambda e, c=c: e.activation(
                yc[:, :], xlnfm[:, c, :], AF.Identity, scale=convsb[:, c, 3:4]),
                reads=[xlnfm_b], sreads=[convsb_b], writes=[yc_b])
            for j in range(1, 4):
                sh = 256 * j
                ctx.emit(ctx.dve, lambda e, c=c, j=j, sh=sh: e.scalar_tensor_tensor(
                    yc[:, sh:TOK], xlnfm[:, c, 0:TOK - sh], convsb[:, c, 3 - j:4 - j],
                    yc[:, sh:TOK], ALU.mult, ALU.add),
                    reads=[xlnfm_b], sreads=[convsb_b], writes=[yc_b])
            ctx.emit(ctx.act, lambda e, c=c: e.activation(xcbf[:, :], yc[:, :], AF.Silu),
                     reads=[yc_b], writes=[xcbf_b])
            ctx.emit(ctx.pool, lambda e, c=c: e.dma_start(
                out=xcfm_d[c * 128:(c + 1) * 128, :], in_=xcbf[:, :]),
                reads=[xcbf_b], inc=16, sem=s_st)

    with ctx.scope() as s2:
        wgsb = ctx.sb(s2, "g_wgsb", [128, 8, D], BF16)
        wgsb_b = Buf("g_wgsb")
        ctx.emit(ctx.pool, lambda e: e.dma_start(
            out=wgsb[:, :, :], in_=wg1.ap().rearrange("(c p) n -> p c n", p=128)),
            writes=[wgsb_b], inc=16, sem=s_wl)
        bgsb = ctx.sb(s2, "g_bgsb", [128, 8], F32)
        bgsb_b = Buf("g_bgsb")
        bgrow = ctx.sb(s2, "g_bgrow", [1, D], F32)
        bgrow_b = Buf("g_bgrow")
        ctx.emit(ctx.sp, lambda e: e.dma_start(out=bgrow[:, :], in_=bg1[:, :]),
                 writes=[bgrow_b], inc=16)
        psA = [ctx.psum(s2, f"g_psA{i}", [128, 512], F32) for i in range(2)]
        psA_b = [Buf(f"g_psA{i}") for i in range(2)]
        for dc in range(8):
            ctx.mm_group([lambda e, dc=dc: e.matmul(
                psA[0][:, 0:1], bgrow[0:1, dc * 128:dc * 128 + 128],
                ones_sb[0:1, 0:1], start=True, stop=True)],
                reads=[bgrow_b, ones_b], writes=[psA_b[0]])
            ctx.emit(ctx.dve, lambda e, dc=dc: e.tensor_copy(bgsb[:, dc:dc + 1], psA[0][:, 0:1]),
                     reads=[psA_b[0]], writes=[bgsb_b])
        xct = [ctx.sb(s2, f"g_xct{i}", [128, 8, 512], BF16) for i in range(2)]
        xct_b = [Buf(f"g_xct{i}") for i in range(2)]
        gt = ctx.sb(s2, "g_gt", [128, 512], BF16)
        gt_b = Buf("g_gt")
        for tr in range(8):
            par = tr % 2
            for k in range(8):
                ctx.emit(ctx.sp, lambda e, k=k, tr=tr, par=par: e.dma_start(
                    out=xct[par][:, k, :], in_=xcfm_d[k * 128:k * 128 + 128,
                                                      tr * 512:tr * 512 + 512]),
                    writes=[xct_b[par]], inc=16, sem=s_sc)
            for dc in range(8):
                pp = dc % 2
                fns = [
                    (lambda e, k=k, dc=dc, pp=pp, par=par: e.matmul(
                        psA[pp][:, :], wgsb[:, k, dc * 128:dc * 128 + 128],
                        xct[par][:, k, :], start=(k == 0), stop=(k == 7)))
                    for k in range(8)
                ]
                ctx.mm_group(fns, reads=[wgsb_b, xct_b[par]], writes=[psA_b[pp]])
                ctx.emit(ctx.act, lambda e, dc=dc, pp=pp: e.activation(
                    gt[:, :], psA[pp][:, :], AF.Silu, bias=bgsb[:, dc:dc + 1]),
                    reads=[psA_b[pp]], sreads=[bgsb_b], writes=[gt_b])
                ctx.emit(ctx.pool, lambda e, dc=dc, tr=tr: e.dma_start(
                    out=gsil_d[dc * 128:dc * 128 + 128, tr * 512:tr * 512 + 512],
                    in_=gt[:, :]),
                    reads=[gt_b], inc=16, sem=s_st)

    for hg in range(2):
        with ctx.scope() as sg:
            qsb = ctx.sb(sg, f"g_qsb{hg}", [128, 2, TOK], BF16)
            qsb_b = Buf(f"g_qsb{hg}")
            ksb = ctx.sb(sg, f"g_ksb{hg}", [128, 2, TOK], BF16)
            ksb_b = Buf(f"g_ksb{hg}")
            vsb = ctx.sb(sg, f"g_vsb{hg}", [128, 4, TOK], BF16)
            vsb_b = Buf(f"g_vsb{hg}")
            augw = ctx.sb(sg, f"g_augw{hg}", [8, TOK], BF16)
            augw_b = Buf(f"g_augw{hg}")
            gkraw = ctx.sb(sg, f"g_gkraw{hg}", [128, 2, TOK], F32)
            gkraw_b = [Buf(f"g_gkraw{hg}_{c}") for c in range(2)]
            psA8 = ctx.psum(sg, f"g_psA8{hg}", [8, 512], F32)
            psA8_b = Buf(f"g_psA8{hg}")

            with ctx.scope() as sa:
                wq1sb = ctx.sb(sa, f"g_wq1sb{hg}", [128, 8, 256], BF16)
                wq1sb_b = Buf("g_wq1sb")
                ctx.emit(ctx.pool, lambda e, hg=hg: e.dma_start(
                    out=wq1sb[:, :, :],
                    in_=wq1[:, 256 * hg:256 * hg + 256].rearrange("(c p) n -> p c n", p=128)),
                    writes=[wq1sb_b], inc=16)
                wkgsb = ctx.sb(sa, f"g_wkgsb{hg}", [128, 8, 256], BF16)
                wkgsb_b = Buf("g_wkgsb")
                ctx.emit(ctx.pool, lambda e, hg=hg: e.dma_start(
                    out=wkgsb[:, :, :],
                    in_=wkg[:, 256 * hg:256 * hg + 256].rearrange("(c p) n -> p c n", p=128)),
                    writes=[wkgsb_b], inc=16)
                wvsb = ctx.sb(sa, f"g_wvsb{hg}", [128, 8, 512], BF16)
                wvsb_b = Buf("g_wvsb")
                ctx.emit(ctx.pool, lambda e, hg=hg: e.dma_start(
                    out=wvsb[:, :, :],
                    in_=wv1[:, 512 * hg:512 * hg + 512].rearrange("(c p) n -> p c n", p=128)),
                    writes=[wvsb_b], inc=16)
                xct2 = [ctx.sb(sa, f"g_xct2{hg}{i}", [128, 8, 512], BF16) for i in range(2)]
                xct2_b = [Buf(f"g_xct2{hg}{i}") for i in range(2)]
                psP = [ctx.psum(sa, f"g_psP{hg}{i}", [128, 512], F32) for i in range(2)]
                psP_b = [Buf(f"g_psP{hg}{i}") for i in range(2)]
                for tr in range(8):
                    par = tr % 2
                    for k in range(8):
                        ctx.emit(ctx.sp, lambda e, k=k, tr=tr, par=par: e.dma_start(
                            out=xct2[par][:, k, :], in_=xcfm_d[k * 128:k * 128 + 128,
                                                              tr * 512:tr * 512 + 512]),
                            writes=[xct2_b[par]], inc=16, sem=s_sc)
                    for cq in range(2):
                        pp = cq
                        fns = [
                            (lambda e, k=k, cq=cq, pp=pp, par=par: e.matmul(
                                psP[pp][:, :], wq1sb[:, k, cq * 128:cq * 128 + 128],
                                xct2[par][:, k, :], start=(k == 0), stop=(k == 7)))
                            for k in range(8)
                        ]
                        ctx.mm_group(fns, reads=[wq1sb_b, xct2_b[par]], writes=[psP_b[pp]])
                        ctx.emit(ctx.act, lambda e, cq=cq, tr=tr, pp=pp: e.activation(
                            qsb[:, cq, tr * 512:tr * 512 + 512], psP[pp][:, :],
                            AF.Identity, scale=RSQ32),
                            reads=[psP_b[pp]], writes=[qsb_b])
                    for cq in range(2):
                        pp = cq
                        fns = [
                            (lambda e, k=k, cq=cq, pp=pp, par=par: e.matmul(
                                psP[pp][:, :], wkgsb[:, k, cq * 128:cq * 128 + 128],
                                xct2[par][:, k, :], start=(k == 0), stop=(k == 7)))
                            for k in range(8)
                        ]
                        ctx.mm_group(fns, reads=[wkgsb_b, xct2_b[par]], writes=[psP_b[pp]])
                        ctx.emit(ctx.act, lambda e, cq=cq, tr=tr, pp=pp: e.activation(
                            gkraw[:, cq, tr * 512:tr * 512 + 512], psP[pp][:, :], AF.Copy),
                            reads=[psP_b[pp]], writes=[gkraw_b[cq]])
                    for cv in range(4):
                        pp = cv % 2
                        fns = [
                            (lambda e, k=k, cv=cv, pp=pp, par=par: e.matmul(
                                psP[pp][:, :], wvsb[:, k, cv * 128:cv * 128 + 128],
                                xct2[par][:, k, :], start=(k == 0), stop=(k == 7)))
                            for k in range(8)
                        ]
                        ctx.mm_group(fns, reads=[wvsb_b, xct2_b[par]], writes=[psP_b[pp]])
                        ctx.emit(ctx.act, lambda e, cv=cv, tr=tr, pp=pp: e.activation(
                            vsb[:, cv, tr * 512:tr * 512 + 512], psP[pp][:, :], AF.Copy),
                            reads=[psP_b[pp]], writes=[vsb_b])
                ga = ctx.sb(sa, f"g_ga{hg}", [128, TOK], F32)
                ga_b = Buf("g_ga")
                gb = ctx.sb(sa, f"g_gb{hg}", [128, TOK], F32)
                gb_b = Buf("g_gb")
                for cq in range(2):
                    G = gkraw[:, cq, :]
                    Gb = gkraw_b[cq]
                    ctx.emit(ctx.act, lambda e, G=G: e.activation(
                        ga[:, :], G, AF.Abs),
                        reads=[Gb], writes=[ga_b])
                    ctx.emit(ctx.act, lambda e: e.activation(gb[:, :], ga[:, :], AF.Exp,
                                                             scale=-1.0),
                             reads=[ga_b], writes=[gb_b])
                    ctx.emit(ctx.dve, lambda e: e.tensor_scalar(
                        gb[:, :], gb[:, :], 1.0, None, ALU.add),
                        writes=[gb_b])
                    ctx.emit(ctx.act, lambda e: e.activation(ga[:, :], gb[:, :], AF.Ln),
                             reads=[gb_b], writes=[ga_b])
                    ctx.emit(ctx.dve, lambda e, G=G: e.tensor_scalar(
                        gb[:, :], G, 0.0, None, ALU.min),
                        reads=[Gb], writes=[gb_b])
                    ctx.emit(ctx.dve, lambda e, G=G: e.tensor_tensor(
                        G, gb[:, :], ga[:, :], ALU.subtract),
                        reads=[gb_b, ga_b], writes=[Gb])
                    ctx.emit(ctx.act, lambda e, G=G: e.activation(
                        ga[:, :], G, AF.Exp, scale=1.0 / 16.0),
                        reads=[Gb], writes=[ga_b])
                    ctx.emit(ctx.dve, lambda e, cq=cq: e.tensor_scalar(
                        ksb[:, cq, :], ga[:, :], -1.0, 1.0, ALU.mult, ALU.add),
                        reads=[ga_b], writes=[ksb_b])
                    for b in range(1, 16):
                        ctx.emit(ctx.dve, lambda e, G=G, b=b: e.tensor_tensor(
                            G[:, b * 256:(b + 1) * 256], G[:, b * 256:(b + 1) * 256],
                            G[:, (b - 1) * 256:b * 256], ALU.add),
                            writes=[Gb])
                augsb = ctx.sb(sa, f"g_augsb{hg}", [128, 2], F32)
                augsb_b = Buf("g_augsb")
                augrow = ctx.sb(sa, f"g_augrow{hg}", [1, 512], F32)
                augrow_b = Buf("g_augrow")
                ctx.emit(ctx.sp, lambda e: e.dma_start(out=augrow[:, :], in_=augb[:, :]),
                         writes=[augrow_b], inc=16)
                for cq in range(2):
                    ctx.mm_group([lambda e, cq=cq, hg=hg: e.matmul(
                        psP[0][:, 0:1], augrow[0:1, 256 * hg + 128 * cq:256 * hg + 128 * cq + 128],
                        ones_sb[0:1, 0:1], start=True, stop=True)],
                        reads=[augrow_b, ones_b], writes=[psP_b[0]])
                    ctx.emit(ctx.dve, lambda e, cq=cq: e.tensor_scalar(
                        augsb[:, cq:cq + 1], psP[0][:, 0:1], 1.0 / RSQ32, None, ALU.mult),
                        reads=[psP_b[0]], writes=[augsb_b])
                prodb = ctx.sb(sa, f"g_prodb{hg}", [128, 512], BF16)
                prodb_b = Buf("g_prodb")
                proda = ctx.sb(sa, f"g_proda{hg}", [128, 512], BF16)
                proda_b = Buf("g_proda")
                for tr in range(8):
                    for cq in range(2):
                        ctx.emit(ctx.dve, lambda e, cq=cq, tr=tr: e.tensor_tensor(
                            prodb[:, :], qsb[:, cq, tr * 512:tr * 512 + 512],
                            ksb[:, cq, tr * 512:tr * 512 + 512], ALU.mult),
                            reads=[qsb_b, ksb_b], writes=[prodb_b])
                        ctx.emit(ctx.act, lambda e, cq=cq: e.activation(
                            proda[:, :], prodb[:, :], AF.Identity,
                            scale=augsb[:, cq:cq + 1]),
                            reads=[prodb_b], sreads=[augsb_b], writes=[proda_b])
                        ctx.mm_group([lambda e, cq=cq: e.matmul(
                            psA8[:, :], eqk_sb[:, cq, :], proda[:, :],
                            start=(cq == 0), stop=(cq == 1), skip_group_check=True)],
                            reads=[eqk_b, proda_b], writes=[psA8_b])
                    ctx.emit(ctx.act, lambda e, tr=tr: e.activation(
                        augw[:, tr * 512:tr * 512 + 512], psA8[:, :], AF.Copy),
                        reads=[psA8_b], writes=[augw_b])
                for cq in range(2):
                    G = gkraw[:, cq, :]
                    Gb = gkraw_b[cq]
                    ctx.emit(ctx.act, lambda e, G=G: e.activation(
                        ga[:, :], G, AF.Exp, scale=1.0 / 16.0),
                        reads=[Gb], writes=[ga_b])
                    ctx.emit(ctx.dve, lambda e, cq=cq: e.tensor_tensor(
                        qsb[:, cq, :], qsb[:, cq, :], ga[:, :], ALU.mult),
                        reads=[ga_b], writes=[qsb_b])
                    ctx.emit(ctx.act, lambda e, G=G: e.activation(
                        gb[:, :], G, AF.Exp, scale=-1.0 / 16.0),
                        reads=[Gb], writes=[gb_b])
                    ctx.emit(ctx.dve, lambda e, cq=cq: e.tensor_tensor(
                        ksb[:, cq, :], ksb[:, cq, :], gb[:, :], ALU.mult),
                        reads=[gb_b], writes=[ksb_b])

            og = ctx.sb(sg, f"g_og{hg}", [128, 4, TOK], F32)
            og_b = [Buf(f"g_og{hg}_{cv}") for cv in range(4)]
            for cv in range(4):
                ctx.emit(ctx.dve, lambda e, cv=cv: e.memset(og[:, cv, :], 0.0),
                         writes=[og_b[cv]])
            prodbf = [ctx.sb(sg, f"g_prodbf{hg}{i}", [128, 1024], BF16) for i in range(2)]
            prodbf_b = [Buf(f"g_prodbf{hg}{i}") for i in range(2)]
            asb = [ctx.sb(sg, f"g_asb{hg}{i}", [8, 1024], BF16) for i in range(2)]
            asb_b = [Buf(f"g_asb{hg}{i}") for i in range(2)]
            tmpf = [ctx.sb(sg, f"g_tmpf{hg}{i}", [128, 1024], F32) for i in range(2)]
            tmpf_b = [Buf(f"g_tmpf{hg}{i}") for i in range(2)]
            fold = [ctx.sb(sg, f"g_fold{hg}{i}", [128, 256], F32) for i in range(2)]
            fold_b = [Buf(f"g_fold{hg}{i}") for i in range(2)]
            psA8w = ctx.psum(sg, f"g_psA8w{hg}", [8, 1024], F32)
            psA8w_b = Buf(f"g_psA8w{hg}")
            psAbw = [ctx.psum(sg, f"g_psAbw{hg}{i}", [128, 1024], F32) for i in range(2)]
            psAbw_b = [Buf(f"g_psAbw{hg}{i}") for i in range(2)]
            pidx = 0
            for t in range(16):
                nb_all = t + 1
                for tb in range((nb_all + 3) // 4):
                    tau0 = tb * 4
                    nb = min(4, nb_all - tau0)
                    w = nb * 256
                    pp = pidx % 2
                    pidx += 1
                    for cq in range(2):
                        qv = qsb[:, cq, t * 256:t * 256 + 256]
                        qbc = bass.AP(qv.tensor, qv.offset,
                                      [list(qv.ap[0])] + [[0, nb]] + [list(qv.ap[-1])])
                        ctx.emit(ctx.dve, lambda e, cq=cq, pp=pp, qbc=qbc, tau0=tau0, nb=nb, w=w:
                                 e.tensor_tensor(
                                     prodbf[pp][:, 0:w].rearrange("p (a b) -> p a b", b=256),
                                     qbc,
                                     ksb[:, cq, tau0 * 256:tau0 * 256 + w].rearrange(
                                         "p (a b) -> p a b", b=256),
                                     ALU.mult),
                                 reads=[qsb_b, ksb_b], writes=[prodbf_b[pp]])
                        fns = []
                        for o5 in range(0, w, 512):
                            w5 = min(512, w - o5)
                            fns.append(lambda e, cq=cq, pp=pp, o5=o5, w5=w5: e.matmul(
                                psA8w[0:8, o5:o5 + w5], eqk_sb[:, cq, :],
                                prodbf[pp][:, o5:o5 + w5],
                                start=(cq == 0), stop=(cq == 1), skip_group_check=True))
                        ctx.mm_group(fns, reads=[eqk_b, prodbf_b[pp]], writes=[psA8w_b])
                    ctx.emit(ctx.act, lambda e, pp=pp, w=w: e.activation(
                        asb[pp][:, 0:w], psA8w[0:8, 0:w], AF.Copy),
                        reads=[psA8w_b], writes=[asb_b[pp]])
                    for cv in range(4):
                        pb = cv % 2
                        fns = []
                        for o5 in range(0, w, 512):
                            w5 = min(512, w - o5)
                            fns.append(lambda e, cv=cv, pp=pp, pb=pb, o5=o5, w5=w5: e.matmul(
                                psAbw[pb][:, o5:o5 + w5], ebcA_sb[:, cv, :],
                                asb[pp][:, o5:o5 + w5], start=True, stop=True))
                        ctx.mm_group(fns, reads=[ebcA_b, asb_b[pp]], writes=[psAbw_b[pb]])
                        ctx.emit(ctx.dve, lambda e, cv=cv, tau0=tau0, pb=pb, w=w:
                                 e.tensor_tensor(
                                     tmpf[pb][:, 0:w], psAbw[pb][:, 0:w],
                                     vsb[:, cv, tau0 * 256:tau0 * 256 + w], ALU.mult),
                                 reads=[psAbw_b[pb], vsb_b], writes=[tmpf_b[pb]])
                        if nb == 1:
                            ctx.emit(ctx.pool, lambda e, cv=cv, t=t, pb=pb: e.tensor_tensor(
                                og[:, cv, t * 256:t * 256 + 256],
                                og[:, cv, t * 256:t * 256 + 256], tmpf[pb][:, 0:256],
                                ALU.add),
                                reads=[tmpf_b[pb]], writes=[og_b[cv]])
                        else:
                            ctx.emit(ctx.dve, lambda e, pb=pb: e.tensor_tensor(
                                fold[pb][:, :], tmpf[pb][:, 0:256], tmpf[pb][:, 256:512],
                                ALU.add),
                                reads=[tmpf_b[pb]], writes=[fold_b[pb]])
                            for ex2 in range(2, nb):
                                ctx.emit(ctx.dve, lambda e, pb=pb, ex2=ex2: e.tensor_tensor(
                                    fold[pb][:, :], fold[pb][:, :],
                                    tmpf[pb][:, ex2 * 256:ex2 * 256 + 256], ALU.add),
                                    reads=[tmpf_b[pb]], writes=[fold_b[pb]])
                            ctx.emit(ctx.pool, lambda e, cv=cv, t=t, pb=pb: e.tensor_tensor(
                                og[:, cv, t * 256:t * 256 + 256],
                                og[:, cv, t * 256:t * 256 + 256], fold[pb][:, :], ALU.add),
                                reads=[fold_b[pb]], writes=[og_b[cv]])
            sgf = ctx.sb(sg, f"g_sgf{hg}", [128, 512], F32)
            sgf_b = Buf(f"g_sgf{hg}")
            sgs = ctx.sb(sg, f"g_sgs{hg}", [128, 512], F32)
            sgs_b = Buf(f"g_sgs{hg}")
            for tr in range(8):
                for cv in range(4):
                    pb = cv % 2
                    ctx.mm_group([lambda e, cv=cv, tr=tr, pb=pb: e.matmul(
                        psAbw[pb][:, 0:512], ebcA_sb[:, cv, :], augw[:, tr * 512:tr * 512 + 512],
                        start=True, stop=True)],
                        reads=[ebcA_b, augw_b], writes=[psAbw_b[pb]])
                    ctx.emit(ctx.dve, lambda e, cv=cv, tr=tr, pb=pb: e.tensor_tensor(
                        sgf[:, :], psAbw[pb][:, 0:512], vsb[:, cv, tr * 512:tr * 512 + 512],
                        ALU.mult),
                        reads=[psAbw_b[pb], vsb_b], writes=[sgf_b])
                    ctx.emit(ctx.act, lambda e: e.activation(sgs[:, :], sgf[:, :], AF.Sigmoid),
                             reads=[sgf_b], writes=[sgs_b])
                    ctx.emit(ctx.pool, lambda e, cv=cv, tr=tr: e.tensor_tensor(
                        og[:, cv, tr * 512:tr * 512 + 512],
                        og[:, cv, tr * 512:tr * 512 + 512], sgs[:, :], ALU.add),
                        reads=[sgs_b], writes=[og_b[cv]])
            psMu = psA8w[0:8, 0:512]
            psMu_b = psA8w_b
            psMs = psA8w[0:8, 512:1024]
            psMs_b = psA8w_b
            psR = psAbw[0][:, 0:512]
            psR_b = psAbw_b[0]
            psM2 = psAbw[1][:, 0:512]
            psM2_b = psAbw_b[1]
            sqf = tmpf[0][:, 0:512]
            sqf_b = tmpf_b[0]
            msb = tmpf[1][0:8, 0:512]
            msb_b = tmpf_b[1]
            msq = tmpf[1][0:8, 512:1024]
            msq_b = tmpf_b[1]
            var = ctx.sb(sg, f"g_var{hg}", [8, 512], F32)
            var_b = Buf(f"g_var{hg}")
            sdv = ctx.sb(sg, f"g_sdv{hg}", [8, 512], F32)
            sdv_b = Buf(f"g_sdv{hg}")
            rst = ctx.sb(sg, f"g_rst{hg}", [8, 512], F32)
            rst_b = Buf(f"g_rst{hg}")
            mrs = ctx.sb(sg, f"g_mrs{hg}", [8, 512], F32)
            mrs_b = Buf(f"g_mrs{hg}")
            onb = prodbf[0][:, 0:512]
            onb_b = prodbf_b[0]
            o1f = ctx.sb(sg, f"g_o1f{hg}", [128, 512], F32)
            o1f_b = Buf(f"g_o1f{hg}")
            for tr in range(8):
                fns = [
                    (lambda e, cv=cv, tr=tr: e.matmul(
                        psMu, esumV_sb[:, cv, :], og[:, cv, tr * 512:tr * 512 + 512],
                        start=(cv == 0), stop=(cv == 3)))
                    for cv in range(4)
                ]
                ctx.mm_group(fns, reads=[esumV_b] + og_b, writes=[psMu_b])
                for cv in range(4):
                    ctx.emit(ctx.dve, lambda e, cv=cv, tr=tr: e.tensor_tensor(
                        sqf, og[:, cv, tr * 512:tr * 512 + 512],
                        og[:, cv, tr * 512:tr * 512 + 512], ALU.mult),
                        reads=[og_b[cv]], writes=[sqf_b])
                    ctx.mm_group([lambda e, cv=cv: e.matmul(
                        psMs, esumV_sb[:, cv, :], sqf,
                        start=(cv == 0), stop=(cv == 3), skip_group_check=True)],
                        reads=[esumV_b, sqf_b], writes=[psMs_b])
                ctx.emit(ctx.act, lambda e: e.activation(msb, psMu, AF.Copy,
                                                         scale=1.0 / 64.0),
                         reads=[psMu_b], writes=[msb_b])
                ctx.emit(ctx.act, lambda e: e.activation(msq, psMu, AF.Square,
                                                         scale=1.0 / 64.0),
                         reads=[psMu_b], writes=[msq_b])
                ctx.emit(ctx.dve, lambda e: e.tensor_scalar(
                    var[:, :], psMs, 1.0 / 64.0, 1e-5, ALU.mult, ALU.add),
                    reads=[psMs_b], writes=[var_b])
                ctx.emit(ctx.dve, lambda e: e.tensor_tensor(
                    var[:, :], var[:, :], msq, ALU.subtract),
                    reads=[msq_b], writes=[var_b])
                ctx.emit(ctx.act, lambda e: e.activation(sdv[:, :], var[:, :], AF.Sqrt),
                         reads=[var_b], writes=[sdv_b])
                ctx.emit(ctx.dve, lambda e: e.reciprocal(rst[:, :], sdv[:, :]),
                         reads=[sdv_b], writes=[rst_b])
                ctx.emit(ctx.dve, lambda e: e.tensor_tensor(
                    mrs[:, :], msb, rst[:, :], ALU.mult),
                    reads=[msb_b, rst_b], writes=[mrs_b])
                for cv in range(4):
                    ctx.mm_group([lambda e, cv=cv: e.matmul(
                        psR, ebcVf_sb[:, cv, :], rst[:, :], start=True, stop=True)],
                        reads=[ebcVf_b, rst_b], writes=[psR_b])
                    ctx.mm_group([lambda e, cv=cv: e.matmul(
                        psM2, ebcVf_sb[:, cv, :], mrs[:, :], start=True, stop=True)],
                        reads=[ebcVf_b, mrs_b], writes=[psM2_b])
                    ctx.emit(ctx.dve, lambda e, cv=cv, tr=tr: e.tensor_tensor(
                        o1f[:, :], og[:, cv, tr * 512:tr * 512 + 512], psR, ALU.mult),
                        reads=[og_b[cv], psR_b], writes=[o1f_b])
                    ctx.emit(ctx.dve, lambda e: e.tensor_tensor(
                        onb, o1f[:, :], psM2, ALU.subtract),
                        reads=[o1f_b, psM2_b], writes=[onb_b])
                    ctx.emit(ctx.pool, lambda e, cv=cv, tr=tr, hg=hg: e.dma_start(
                        out=on_d[hg * 512 + cv * 128:hg * 512 + cv * 128 + 128,
                                 tr * 512:tr * 512 + 512],
                        in_=onb),
                        reads=[onb_b], inc=16, sem=s_st)

    with ctx.scope() as s4:
        wosb = ctx.sb(s4, "g_wosb", [128, 8, D], BF16)
        wosb_b = Buf("g_wosb")
        ctx.emit(ctx.pool, lambda e: e.dma_start(
            out=wosb[:, :, :], in_=wout.ap().rearrange("(c p) n -> p c n", p=128)),
            writes=[wosb_b], inc=16)
        gst = [ctx.sb(s4, f"g_gst{i}", [128, 8, 512], BF16) for i in range(2)]
        gst_b = [Buf(f"g_gst{i}") for i in range(2)]
        ont = [ctx.sb(s4, f"g_ont{i}", [128, 8, 512], BF16) for i in range(2)]
        ont_b = [Buf(f"g_ont{i}") for i in range(2)]
        gated = ctx.sb(s4, "g_gated", [128, 8, 512], BF16)
        gated_b = Buf("g_gated")
        xt4 = [ctx.sb(s4, f"g_xt4{i}", [128, D], F32) for i in range(2)]
        xt4_b = [Buf(f"g_xt4{i}") for i in range(2)]
        xn4 = [ctx.sb(s4, f"g_xn4{i}", [128, D], F32) for i in range(2)]
        xn4_b = [Buf(f"g_xn4{i}") for i in range(2)]
        psF = [ctx.psum(s4, f"g_psF{i}", [128, 512], F32) for i in range(2)]
        psF_b = [Buf(f"g_psF{i}") for i in range(2)]
        for tr in range(8):
            par = tr % 2
            for c8 in range(8):
                ctx.emit(ctx.sp, lambda e, c8=c8, tr=tr, par=par: e.dma_start(
                    out=gst[par][:, c8, :], in_=gsil_d[c8 * 128:c8 * 128 + 128,
                                                       tr * 512:tr * 512 + 512]),
                    writes=[gst_b[par]], inc=16, sem=s_sc)
                ctx.emit(ctx.sp, lambda e, c8=c8, tr=tr, par=par: e.dma_start(
                    out=ont[par][:, c8, :], in_=on_d[c8 * 128:c8 * 128 + 128,
                                                     tr * 512:tr * 512 + 512]),
                    writes=[ont_b[par]], inc=16, sem=s_sc)
            ctx.emit(ctx.dve, lambda e, par=par: e.tensor_tensor(
                gated[:, :, :], gst[par][:, :, :], ont[par][:, :, :], ALU.mult),
                reads=[gst_b[par], ont_b[par]], writes=[gated_b])
            for s4i in range(4):
                sub = tr * 4 + s4i
                r0 = sub * 128
                spar = s4i % 2
                ctx.emit(ctx.sp, lambda e, r0=r0, spar=spar: e.dma_start(
                    out=xt4[spar][:, :], in_=xsrc[r0:r0 + 128, :]),
                    writes=[xt4_b[spar]], inc=16, sem=s_xl)
                for half in range(2):
                    pp = half
                    fns = [
                        (lambda e, c8=c8, s4i=s4i, half=half, pp=pp: e.matmul(
                            psF[pp][:, :], gated[:, c8, s4i * 128:s4i * 128 + 128],
                            wosb[:, c8, half * 512:half * 512 + 512],
                            start=(c8 == 0), stop=(c8 == 7)))
                        for c8 in range(8)
                    ]
                    ctx.mm_group(fns, reads=[gated_b, wosb_b], writes=[psF_b[pp]])
                    ctx.emit(ctx.dve, lambda e, half=half, pp=pp, spar=spar: e.tensor_tensor(
                        xn4[spar][:, half * 512:half * 512 + 512], psF[pp][:, :],
                        xt4[spar][:, half * 512:half * 512 + 512], ALU.add),
                        reads=[psF_b[pp], xt4_b[spar]], writes=[xn4_b[spar]])
                ctx.emit(ctx.pool, lambda e, r0=r0, spar=spar: e.dma_start(
                    out=xdst[r0:r0 + 128, :], in_=xn4[spar][:, :]),
                    reads=[xn4_b[spar]], inc=16, sem=s_st)


def _runA(inputs, xa):
    nc = build_launch_A()
    w1 = np.ascontiguousarray(inputs["L0_ffn_W1"].T)
    w2 = np.ascontiguousarray(inputs["L0_ffn_W2"].T)
    common = {
        "w1t": w1, "w2t": w2,
        "b1": inputs["L0_ffn_b1"].reshape(1, -1),
        "b2": inputs["L0_ffn_b2"].reshape(1, -1),
        "wk": inputs["L0_attn_Wk"], "wq": inputs["L0_attn_Wq"],
        "wo": inputs["L0_attn_Wo"], "bo": inputs["L0_attn_bo"].reshape(1, -1),
        "ident": np.eye(128, dtype=ml_dtypes.bfloat16),
        "onesf": np.ones((128, 128), np.float32),
    }
    in_maps = []
    for r in range(R):
        m = dict(common)
        m["xin"] = np.ascontiguousarray(xa[2 * r:2 * r + 2].reshape(TOK, D))
        in_maps.append(m)
    res = run_bass_kernel_spmd(nc, in_maps, core_ids=list(range(R)))
    out = np.empty((16, 2048, D), np.float32)
    for r in range(R):
        out[2 * r:2 * r + 2] = res.results[r]["xout"].reshape(2, 2048, D)
    return out


def _gla_consts():
    eqk = np.zeros((2 * 128, 8), dtype=ml_dtypes.bfloat16)
    for c in range(2):
        for r in range(128):
            eqk[c * 128 + r, 4 * c + r // 32] = 1
    ebcA = np.zeros((4 * 8, 128), dtype=ml_dtypes.bfloat16)
    esumV = np.zeros((4 * 128, 8), dtype=np.float32)
    ebcVf = np.zeros((4 * 8, 128), dtype=np.float32)
    for c in range(4):
        for p in range(128):
            h = 2 * c + p // 64
            ebcA[c * 8 + h, p] = 1
            ebcVf[c * 8 + h, p] = 1
            esumV[c * 128 + p, h] = 1
    return eqk, ebcA, esumV, ebcVf


def _runB(inputs, xb):
    nc = build_launch_B()
    eqk, ebcA, esumV, ebcVf = _gla_consts()
    common = {
        "wgm": np.ascontiguousarray(inputs["L1_ffn_Wg"].T),
        "w1m": np.ascontiguousarray(
            inputs["L1_ffn_W1"].reshape(8, 8, 256, D).transpose(0, 1, 3, 2)
        ).reshape(8 * 8 * D, 256),
        "w2m": np.ascontiguousarray(
            inputs["L1_ffn_W2"].reshape(8, 8, 256, D).transpose(0, 1, 3, 2)
        ).reshape(8 * 8 * D, 256),
        "w3m": np.ascontiguousarray(inputs["L1_ffn_W3"].transpose(0, 2, 1)).reshape(8 * 2048, D),
        "convw": inputs["L1_attn_conv_w"],
        "wq1": inputs["L1_attn_Wq"], "wkg": inputs["L1_attn_Wkg"],
        "wv1": inputs["L1_attn_Wv"], "wg1": inputs["L1_attn_Wg"],
        "bg1": inputs["L1_attn_bg"].reshape(1, -1),
        "wout": inputs["L1_attn_Wout"],
        "augb": inputs["L1_attn_aug_balance"].reshape(1, -1),
        "ident": np.eye(128, dtype=ml_dtypes.bfloat16),
        "onesf": np.ones((128, 128), np.float32),
        "onesb": np.ones((128, 128), ml_dtypes.bfloat16),
        "identf": np.eye(128, dtype=np.float32),
        "eqk": eqk, "ebcA": ebcA, "esumV": esumV, "ebcVf": ebcVf,
    }
    in_maps = []
    for r in range(R):
        m = dict(common)
        m["xin"] = np.ascontiguousarray(
            xb[:, 256 * r:256 * (r + 1), :]).reshape(TOK, D)
        in_maps.append(m)
    res = run_bass_kernel_spmd(nc, in_maps, core_ids=list(range(R)))
    out = np.empty((16, 2048, D), np.float32)
    for r in range(R):
        out[:, 256 * r:256 * (r + 1), :] = res.results[r]["xout"].reshape(16, 256, D)
    return out


def kernel(**inputs):
    inputs = {k: np.asarray(v) for k, v in inputs.items()}
    xa = inputs["L0_attn_x"]
    x_afterA = _runA(inputs, xa)
    return _runB(inputs, x_afterA)
